# version 20
# baseline (speedup 1.0000x reference)
import numpy as np
import ml_dtypes
from collections import defaultdict
from contextlib import ExitStack

import concourse.bass as bass
import concourse.mybir as mybir
from concourse.bass_utils import run_bass_kernel_spmd

F32 = mybir.dt.float32
BF16 = mybir.dt.bfloat16
I32 = mybir.dt.int32
AF = mybir.ActivationFunctionType
ALU = mybir.AluOpType
AX = mybir.AxisListType
IOA = bass.IndirectOffsetOnAxis

DECAY = 0.99
TB = 512
DM = 1024
NEXP = 8

ENGINES = ("sync", "tensor", "vector", "scalar", "gpsimd")


class Res:
    __slots__ = ("name", "writer", "readers", "dma_queue")

    def __init__(self, name):
        self.name = name
        self.writer = None
        self.readers = []
        self.dma_queue = None


class Prog:

    def __init__(self, nc):
        self.nc = nc
        self.ops = {e: [] for e in ENGINES}
        self.cnt = defaultdict(int)
        self.floors = defaultdict(dict)
        self.res = {}
        self.sems = {}

    def r(self, name):
        if name not in self.res:
            self.res[name] = Res(name)
        return self.res[name]

    def op(self, eng, emit, reads=(), writes=(), dma=False, dma_sem=None):
        reads = [self.r(x) if not isinstance(x, Res) else x for x in reads]
        writes = [self.r(x) if not isinstance(x, Res) else x for x in writes]
        waits = {}

        def add(tok):
            if tok is None:
                return
            sem, val = tok
            if eng == "tensor" and sem == "Stensor":
                return
            if self.floors[eng].get(sem, 0) >= val:
                return
            waits[sem] = max(waits.get(sem, 0), val)

        for rr in reads:
            add(rr.writer)
        for ww in writes:
            add(ww.writer)
            for rd in ww.readers:
                add(rd)
        for sem, val in waits.items():
            self.floors[eng][sem] = val
        if dma:
            assert len(writes) >= 1
            kind = "sw" if eng in ("gpsimd", "vector") else "hw"
            w0 = writes[0]
            if w0.dma_queue is None:
                w0.dma_queue = kind
            assert w0.dma_queue == kind, f"res {w0.name}: SWDGE/HWDGE sem conflict"
            sem_name, amt = "B" + (dma_sem if dma_sem else w0.name), 16
        else:
            sem_name, amt = "S" + eng, 1
        self.cnt[sem_name] += amt
        tok = (sem_name, self.cnt[sem_name])
        for ww in writes:
            ww.writer = tok
            ww.readers = []
        for rr in reads:
            rr.readers.append(tok)
        self.ops[eng].append((sorted(waits.items()), emit, sem_name, amt))
        return tok

    def settle(self, res_names, tok):
        for rn in res_names:
            self.r(rn).writer = tok

    def barrier(self):
        toks = [(s, c) for s, c in self.cnt.items() if c > 0]
        for e in ENGINES:
            waits = {}
            for sem, val in toks:
                if self.floors[e].get(sem, 0) >= val:
                    continue
                waits[sem] = max(waits.get(sem, 0), val)
            for sem, val in waits.items():
                self.floors[e][sem] = val
            if waits:
                self.ops[e].append((sorted(waits.items()), None, None, 0))

    def emit(self, exitstack):
        used = {s for s, c in self.cnt.items() if c > 0}
        for s in sorted(used):
            self.sems[s] = exitstack.enter_context(self.nc.semaphore(s))
        block = exitstack.enter_context(self.nc.Block())
        prog = self

        def section(eng_name):
            def body(eng):
                for waits, emit, sem_name, amt in prog.ops[eng_name]:
                    for wsem, wval in waits:
                        if wsem in prog.sems:
                            eng.wait_ge(prog.sems[wsem], wval)
                    if emit is not None:
                        inst = emit(eng)
                        if sem_name is not None:
                            inst.then_inc(prog.sems[sem_name], amt)
            return body

        for eng_name in ENGINES:
            if prog.ops[eng_name]:
                getattr(block, eng_name)(section(eng_name))
        return block



def _lhsT_tiles(w):
    return np.ascontiguousarray(
        w.reshape(8, 128, 8, 128).transpose(1, 0, 2, 3).reshape(128, 8192))


def _rhs_tiles(w):
    return np.ascontiguousarray(
        w.reshape(8, 128, 1024).transpose(1, 0, 2).reshape(128, 8192))


def _w13_tiles(w):
    return np.ascontiguousarray(
        w.reshape(-1, 8, 128, 8, 128).transpose(0, 3, 2, 1, 4).reshape(-1, 8, 128, 1024))


def _w2_tiles(w):
    return np.ascontiguousarray(
        w.reshape(-1, 8, 128, 1024).transpose(0, 2, 1, 3).reshape(-1, 128, 8192))


def host_constants():
    p = np.arange(128, dtype=np.float64)
    t512 = np.arange(TB, dtype=np.float64)
    cdec = np.zeros((128, 4 * TB), dtype=np.float32)
    for sc in range(4):
        s_abs = sc * 128 + p[:, None]
        m = t512[None, :] >= s_abs
        cdec[:, sc * TB:(sc + 1) * TB] = (m * DECAY ** (t512[None, :] - s_abs)).astype(np.float32)
    cdrow = np.ascontiguousarray(
        np.broadcast_to((DECAY ** t512).astype(np.float32), (128, TB)))
    cvw = np.zeros((128, 4), dtype=np.float32)
    for sub in range(4):
        cvw[:, sub] = (1.0 - DECAY) * DECAY ** (TB - (sub * 128 + p))
    ciota = np.ascontiguousarray(np.broadcast_to(np.arange(8, dtype=np.float32), (128, 8)))
    k = np.arange(128)
    cltri = (k[:, None] < k[None, :]).astype(np.float32)
    return {
        "cdec": cdec, "cdrow": cdrow, "cvw": cvw, "ciota": ciota,
        "cones": np.ones((1, 128), dtype=np.float32),
        "conesc": np.ones((128, 1), dtype=np.float32),
        "cident": np.eye(128, dtype=np.float32),
        "cltri": np.ascontiguousarray(cltri),
    }



def build_nc(T=4096):
    NBLK = T // TB
    NTILE = T // 512
    NSUB = T // 128
    NGRP = T // 1024
    GC = 1024
    D512 = float(DECAY ** TB)
    CAP = ((5 * T // 16 + 127) // 128) * 128
    TRASH = NEXP * CAP
    NCS = CAP // 128
    csplits = []
    off = 0
    while off < CAP:
        w = min(512, CAP - off)
        csplits.append((off, w))
        off += w

    nc = bass.Bass("TRN2", target_bir_lowering=False, debug=False)
    dram = {}
    for name, shape in [
        ("x", [T, DM]), ("wq", [128, 8192]), ("wk", [128, 8192]),
        ("wv", [128, 8192]), ("wo", [128, 8192]), ("bq", [128, 8]),
        ("bk", [128, 8]), ("gw", [128, 64]),
        ("w1", [NEXP, 8, 128, 1024]), ("w3", [NEXP, 8, 128, 1024]),
        ("w2", [NEXP, 128, 8192]),
        ("w1s", [8, 128, 1024]), ("w3s", [8, 128, 1024]), ("w2s", [128, 8192]),
        ("cdec", [128, 2048]), ("cdrow", [128, 512]), ("cvw", [128, 4]),
        ("ciota", [128, 8]), ("cones", [1, 128]), ("conesc", [128, 1]),
        ("cident", [128, 128]), ("cltri", [128, 128]),
    ]:
        dram[name] = nc.dram_tensor(name, shape, F32, kind="ExternalInput").ap()
    out_d = nc.dram_tensor("out", [T, DM], F32, kind="ExternalOutput").ap()
    xnd = nc.dram_tensor("xnd", [T, DM], BF16).ap()
    shd = nc.dram_tensor("shd", [T, DM], BF16).ap()
    yd = nc.dram_tensor("yd", [128, 8 * T], BF16).ap()
    xbkt = nc.dram_tensor("xbkt", [NEXP * CAP + 1, DM], BF16).ap()
    obkt = nc.dram_tensor("obkt", [NEXP * CAP + 1, DM], BF16).ap()

    with ExitStack() as S0:
        E0 = S0.enter_context
        P = Prog(nc)

        ps = [E0(nc.psum_tensor(f"ps{i}", [128, 512], F32)) for i in range(6)]
        pbf = E0(nc.psum_tensor("pbf", [128, 1024], BF16))
        pbf2 = E0(nc.psum_tensor("pbf2", [128, 1024], BF16))
        PS = [f"PS{i}" for i in range(6)]
        PBF = "PBF"
        PBF2 = "PBF2"

        cdec = E0(nc.sbuf_tensor("cdec_s", [128, 2048], F32))
        cdrow = E0(nc.sbuf_tensor("cdrow_s", [128, 512], F32))
        cvw = E0(nc.sbuf_tensor("cvw_s", [128, 4], F32))
        ciota = E0(nc.sbuf_tensor("ciota_s", [128, 8], F32))
        cones = E0(nc.sbuf_tensor("cones_s", [1, 128], F32))
        conesc = E0(nc.sbuf_tensor("conesc_s", [128, 1], F32))
        cltri = E0(nc.sbuf_tensor("cltri_s", [128, 128], F32))
        cidentb = E0(nc.sbuf_tensor("cidentb_s", [128, 128], BF16))
        bqc = E0(nc.sbuf_tensor("bqc_s", [128, 8], F32))
        bkc = E0(nc.sbuf_tensor("bkc_s", [128, 8], F32))
        for dn, tt, rn in [("cdec", cdec, "Ccdec"), ("cdrow", cdrow, "Ccdrow"),
                           ("cvw", cvw, "Ccvw"), ("ciota", ciota, "Cciota"),
                           ("cones", cones, "Ccones"), ("conesc", conesc, "Cconesc"),
                           ("cltri", cltri, "Ccltri"),
                           ("bq", bqc, "Cbq"), ("bk", bkc, "Cbk")]:
            P.op("sync", lambda e, tt=tt, dn=dn: e.dma_start(out=tt[:], in_=dram[dn]),
                 writes=[rn], dma=True)
        P.op("gpsimd", lambda e: e.dma_start(out=cidentb[:], in_=dram["cident"]),
             writes=["Cidb"], dma=True)
        gwsb = E0(nc.sbuf_tensor("gwsb", [128, 64], BF16))
        P.op("gpsimd", lambda e: e.dma_start(out=gwsb[:], in_=dram["gw"]),
             writes=["Cgw"], dma=True)
        cols = E0(nc.sbuf_tensor("cols", [128, 144], F32))
        scr = E0(nc.sbuf_tensor("scr", [128, 1024], F32))
        eids = E0(nc.sbuf_tensor("eids", [128, 2 * NSUB], F32))
        scl = E0(nc.sbuf_tensor("scl", [128, 2 * NSUB], F32))
        gidxi = E0(nc.sbuf_tensor("gidxi", [128, 2 * NSUB], I32))
        baserow = E0(nc.sbuf_tensor("baserow", [1, 16], F32))
        bases = E0(nc.sbuf_tensor("bases", [1, 64], F32))
        zrow = E0(nc.sbuf_tensor("zrow", [1, 1024], BF16))
        P.op("vector", lambda e: e.memset(baserow[0:1, 0:8], 0.0), writes=["baserow"])
        P.op("vector", lambda e: e.memset(zrow[:], 0.0), writes=["zrow"])
        P.op("scalar", lambda e: e.dma_start(out=obkt[TRASH:TRASH + 1, :], in_=zrow[:]),
             reads=["zrow"], writes=["obkt"], dma=True)

        with ExitStack() as SA:
            EA = SA.enter_context
            wqsb = EA(nc.sbuf_tensor("wqsb", [128, 8192], BF16))
            wksb = EA(nc.sbuf_tensor("wksb", [128, 8192], BF16))
            wvsb = EA(nc.sbuf_tensor("wvsb", [128, 8192], BF16))
            for nm, tt, rn in [("wq", wqsb, "Wq"), ("wk", wksb, "Wk"), ("wv", wvsb, "Wv")]:
                P.op("gpsimd", lambda e, tt=tt, nm=nm: e.dma_start(out=tt[:], in_=dram[nm]),
                     writes=[rn], dma=True)
            xnf = [[EA(nc.sbuf_tensor(f"xnf{b}_{dc}", [128, 512], BF16)) for dc in range(8)]
                   for b in range(2)]
            xnb = [[EA(nc.sbuf_tensor(f"xnb{b}_{dc}", [128, 512], BF16)) for dc in range(8)]
                   for b in range(2)]
            xst = [EA(nc.sbuf_tensor(f"xst{b}", [128, 1024], F32)) for b in range(2)]
            xnt = [EA(nc.sbuf_tensor(f"xnt{b}", [128, 1024], BF16)) for b in range(4)]
            sq = [EA(nc.sbuf_tensor(f"sq{i}", [128, 1024], F32)) for i in range(2)]
            escr = EA(nc.sbuf_tensor("escr", [128, 32], F32))
            xrt = [EA(nc.sbuf_tensor(f"xrt{b}", [128, 1024], BF16)) for b in range(2)]
            sxn = [EA(nc.sbuf_tensor(f"sxn{b}", [128, 1024], BF16)) for b in range(2)]
            qfm = [EA(nc.sbuf_tensor(f"qfm{dc}", [128, 512], BF16)) for dc in range(8)]
            kfm = [EA(nc.sbuf_tensor(f"kfm{dc}", [128, 512], BF16)) for dc in range(8)]
            vtm = [EA(nc.sbuf_tensor(f"vtm{s}", [128, 1024], BF16)) for s in range(4)]
            vw = [EA(nc.sbuf_tensor(f"vw{s}", [128, 1024], BF16)) for s in range(4)]
            ktm = [EA(nc.sbuf_tensor(f"ktm{s}", [128, 1024], BF16)) for s in range(4)]
            aTs = [EA(nc.sbuf_tensor(f"aTs{s}", [128, 512], BF16)) for s in range(4)]
            yfm = [EA(nc.sbuf_tensor(f"yfm{dc}", [128, 512], BF16)) for dc in range(8)]
            state = [EA(nc.sbuf_tensor(f"state{dc}", [128, 1024], BF16)) for dc in range(8)]
            kvst = [EA(nc.sbuf_tensor(f"kvst{b}", [128, 512], BF16)) for b in range(2)]

            zsrc = EA(nc.sbuf_tensor("zsrc", [128, 1024], BF16))
            P.op("vector", lambda e: e.memset(zsrc[:], 0.0), writes=["zsrc"])
            r0z = 0
            while r0z < NEXP * CAP:
                nr = min(128, NEXP * CAP - r0z)
                dst = xbkt[r0z:r0z + nr, :]
                P.op("gpsimd", lambda e, dst=dst, nr=nr: e.dma_start(out=dst, in_=zsrc[:, 0:nr * 8]),
                     reads=["zsrc"], writes=["xbkt"], dma=True)
                r0z += nr

            def emit_ln_sub(sub):
                tile = sub // 4
                b = sub % 2
                q = sub % 4
                bt = sub % 4
                xrows = dram["x"][sub * 128:(sub + 1) * 128, :]
                P.op("sync", lambda e, b=b, xrows=xrows: e.dma_start(out=xst[b][:], in_=xrows),
                     writes=[f"xst{b}"], dma=True)
                cc = lambda i: cols[:, 16 * q + i:16 * q + i + 1]
                rn = lambda i: f"c{i}_{q}"
                xres = f"xst{b}"
                P.op("vector", lambda e, b=b: e.reduce_sum(cc(0), xst[b][:], axis=AX.X),
                     reads=[xres], writes=[rn(0)])
                P.op("scalar", lambda e, b=b, q=q: e.activation(
                    sq[q % 2][:], xst[b][:], AF.Square, accum_out=cc(1)),
                     reads=[xres], writes=[f"sq{q % 2}", rn(1)])
                P.op("vector", lambda e: e.tensor_scalar(cc(2), cc(0), 1.0 / DM, None, ALU.mult),
                     reads=[rn(0)], writes=[rn(2)])
                P.op("vector", lambda e: e.tensor_scalar(cc(3), cc(1), 1.0 / DM, None, ALU.mult),
                     reads=[rn(1)], writes=[rn(3)])
                P.op("vector", lambda e: e.tensor_tensor(cc(4), cc(2), cc(2), ALU.mult),
                     reads=[rn(2)], writes=[rn(4)])
                P.op("vector", lambda e: e.tensor_tensor(cc(5), cc(3), cc(4), ALU.subtract),
                     reads=[rn(3), rn(4)], writes=[rn(5)])
                P.op("vector", lambda e: e.tensor_scalar(cc(5), cc(5), 1e-5, None, ALU.add),
                     reads=[rn(5)], writes=[rn(5)])
                P.op("scalar", lambda e: e.activation(cc(6), cc(5), AF.Sqrt),
                     reads=[rn(5)], writes=[rn(6)])
                P.op("vector", lambda e: e.reciprocal(cc(7), cc(6)),
                     reads=[rn(6)], writes=[rn(7)])
                P.op("vector", lambda e: e.tensor_scalar(
                    cc(8), cc(2), cc(7), -1.0, ALU.mult, ALU.mult),
                     reads=[rn(2), rn(7)], writes=[rn(8)])
                P.op("scalar", lambda e, b=b, bt=bt: e.activation(
                    xnt[bt][:], xst[b][:], AF.Identity, bias=cc(8), scale=cc(7)),
                     reads=[xres, rn(7), rn(8)], writes=[f"xnt{bt}"])
                orows = xnd[sub * 128:(sub + 1) * 128, :]
                P.op("scalar", lambda e, bt=bt, orows=orows: e.dma_start(out=orows, in_=xnt[bt][:]),
                     reads=[f"xnt{bt}"], writes=[f"xnd{tile}"], dma=True)



            def emit_ln_tr(sub):
                tile = sub // 4
                bt = sub % 4
                pb = pbf if sub % 2 == 0 else pbf2
                pbr = PBF if sub % 2 == 0 else PBF2
                for dc in range(8):
                    P.op("tensor", lambda e, pb=pb, bt=bt, dc=dc: e.transpose(
                        pb[:, dc * 128:(dc + 1) * 128], xnt[bt][:, dc * 128:(dc + 1) * 128],
                        cidentb[:]),
                         reads=[f"xnt{bt}", "Cidb"], writes=[pbr])
                for dc in range(8):
                    dstap = xnf[tile % 2][dc][:, (sub % 4) * 128:(sub % 4 + 1) * 128]
                    P.op("vector", lambda e, pb=pb, dc=dc, dstap=dstap: e.tensor_copy(
                        dstap, pb[:, dc * 128:(dc + 1) * 128]),
                         reads=[pbr], writes=[f"xnf{tile % 2}_{dc}"])

            def emit_gate_sub(sub):
                tile = sub // 4
                q = sub % 4
                B0 = 64 + 16 * q
                G0 = 128 * q
                pg, pgr = ps[sub % 4], PS[sub % 4]
                s1ap = scl[:, 2 * sub:2 * sub + 1]
                s2ap = scl[:, 2 * sub + 1:2 * sub + 2]
                e1ap = eids[:, 2 * sub:2 * sub + 1]
                e2ap = eids[:, 2 * sub + 1:2 * sub + 2]
                cc = lambda i: cols[:, B0 + i:B0 + i + 1]
                rn = lambda i: f"g{i}_{q}"
                sl = lambda a, bnd: scr[:, G0 + a:G0 + bnd]
                sres = lambda nm: f"{nm}_{q}"
                for dc in range(8):
                    P.op("tensor", lambda e, pg=pg, sub=sub, dc=dc: e.matmul(
                        pg[:, 0:8], xnf[tile % 2][dc][:, (sub % 4) * 128:(sub % 4 + 1) * 128],
                        gwsb[:, dc * 8:(dc + 1) * 8], start=(dc == 0), stop=(dc == 7)),
                         reads=[f"xnf{tile % 2}_{dc}", "Cgw"], writes=[pgr])
                P.op("vector", lambda e, pg=pg, cc=cc: e.reduce_max(cc(8), pg[:, 0:8], axis=AX.X, negate=True),
                     reads=[pgr], writes=[rn(8)])
                P.op("scalar", lambda e, pg=pg, cc=cc, q=q: e.activation(escr[:, 8 * q:8 * q + 8], pg[:, 0:8], AF.Exp, bias=cc(8)),
                     reads=[pgr, rn(8)], writes=[sres("escr")])
                P.op("vector", lambda e, cc=cc, q=q: e.reduce_sum(cc(9), escr[:, 8 * q:8 * q + 8], axis=AX.X),
                     reads=[sres("escr")], writes=[rn(9)])
                P.op("vector", lambda e, cc=cc: e.reciprocal(cc(10), cc(9)),
                     reads=[rn(9)], writes=[rn(10)])
                P.op("vector", lambda e, cc=cc, q=q, sl=sl: e.tensor_scalar(sl(0, 8), escr[:, 8 * q:8 * q + 8], cc(10), None, ALU.mult),
                     reads=[sres("escr"), rn(10)], writes=[sres("sc8")])
                P.op("vector", lambda e, s1ap=s1ap, sl=sl: e.reduce_max(s1ap, sl(0, 8), axis=AX.X),
                     reads=[sres("sc8")], writes=[f"scl{q}"])
                P.op("vector", lambda e, s1ap=s1ap, sl=sl: e.tensor_scalar(sl(8, 16), sl(0, 8), s1ap, None, ALU.is_equal),
                     reads=[sres("sc8"), f"scl{q}"], writes=[sres("m1")])
                P.op("vector", lambda e, sl=sl: e.scalar_tensor_tensor(sl(16, 24), sl(8, 16), -8.0, ciota[:], ALU.mult, ALU.add),
                     reads=[sres("m1"), "Cciota"], writes=[sres("cand")])
                P.op("vector", lambda e, cc=cc, sl=sl: e.tensor_reduce(cc(12), sl(16, 24), AX.X, ALU.min),
                     reads=[sres("cand")], writes=[rn(12)])
                P.op("vector", lambda e, e1ap=e1ap, cc=cc: e.tensor_scalar(e1ap, cc(12), 8.0, None, ALU.add),
                     reads=[rn(12)], writes=[f"eids{q}"])
                P.op("vector", lambda e, e1ap=e1ap, sl=sl: e.tensor_scalar(sl(8, 16), ciota[:], e1ap, None, ALU.is_equal),
                     reads=["Cciota", f"eids{q}"], writes=[sres("m1")])
                P.op("vector", lambda e, sl=sl: e.scalar_tensor_tensor(sl(16, 24), sl(8, 16), -1e30, sl(0, 8), ALU.mult, ALU.add),
                     reads=[sres("m1"), sres("sc8")], writes=[sres("cand")])
                P.op("vector", lambda e, s2ap=s2ap, sl=sl: e.reduce_max(s2ap, sl(16, 24), axis=AX.X),
                     reads=[sres("cand")], writes=[f"scl{q}"])
                P.op("vector", lambda e, s2ap=s2ap, sl=sl: e.tensor_scalar(sl(24, 32), sl(16, 24), s2ap, None, ALU.is_equal),
                     reads=[sres("cand"), f"scl{q}"], writes=[sres("m2")])
                P.op("vector", lambda e, sl=sl: e.scalar_tensor_tensor(sl(24, 32), sl(24, 32), -8.0, ciota[:], ALU.mult, ALU.add),
                     reads=[sres("m2"), "Cciota"], writes=[sres("m2")])
                P.op("vector", lambda e, cc=cc, sl=sl: e.tensor_reduce(cc(14), sl(24, 32), AX.X, ALU.min),
                     reads=[sres("m2")], writes=[rn(14)])
                P.op("vector", lambda e, e2ap=e2ap, cc=cc: e.tensor_scalar(e2ap, cc(14), 8.0, None, ALU.add),
                     reads=[rn(14)], writes=[f"eids{q}"])

            def emit_route_oh(sub):
                for r in range(2):
                    col = 2 * sub + r
                    ci = (sub % 4) * 2 + r
                    eap = eids[:, col:col + 1]
                    P.op("vector", lambda e, eap=eap, sub=sub, ci=ci: e.tensor_scalar(
                        scr[:, 512 + ci * 8:512 + ci * 8 + 8], ciota[:], eap, None, ALU.is_equal),
                         reads=["Cciota", f"eids{sub % 4}"], writes=[f"oh{ci}"])

            def emit_route_tile(tile):
                subs = list(range(4 * tile, 4 * tile + 4))
                chunks = [(sub, r) for sub in subs for r in range(2)]
                for ci in range(8):
                    P.op("tensor", lambda e, ci=ci: e.matmul(
                        ps[4][:, ci * 8:ci * 8 + 8], cltri[:], scr[:, 512 + ci * 8:512 + ci * 8 + 8],
                        start=(ci == 0), stop=False),
                         reads=["Ccltri", f"oh{ci}"], writes=[PS[4]])
                    P.op("tensor", lambda e, ci=ci: e.matmul(
                        ps[5][0:1, ci * 8:ci * 8 + 8], conesc[:], scr[:, 512 + ci * 8:512 + ci * 8 + 8],
                        start=(ci == 0), stop=(ci == 7)),
                         reads=["Cconesc", f"oh{ci}"], writes=[PS[5]])
                for ci in range(8):
                    P.op("vector", lambda e, ci=ci: e.tensor_copy(
                        bases[0:1, ci * 8:ci * 8 + 8], baserow[0:1, 0:8]),
                         reads=["baserow"], writes=["bases"])
                    P.op("vector", lambda e, ci=ci: e.tensor_tensor(
                        baserow[0:1, 0:8], baserow[0:1, 0:8], ps[5][0:1, ci * 8:ci * 8 + 8], ALU.add),
                         reads=[PS[5], "baserow"], writes=["baserow"])
                for ci in range(8):
                    P.op("tensor", lambda e, ci=ci: e.matmul(
                        ps[4][:, ci * 8:ci * 8 + 8], cones[:], bases[0:1, ci * 8:ci * 8 + 8],
                        start=False, stop=(ci == 7)),
                         reads=["Ccones", "bases"], writes=[PS[4]])
                for ci, (sub, r) in enumerate(chunks):
                    col = 2 * sub + r
                    b = sub % 2
                    sap = scl[:, col:col + 1]
                    if r == 0:
                        xrows = xnd[sub * 128:(sub + 1) * 128, :]
                        P.op("sync", lambda e, b=b, xrows=xrows: e.dma_start(out=xrt[b][:], in_=xrows),
                             reads=[f"xnd{tile}"], writes=[f"xrt{b}"], dma=True)
                    cq = ci % 4
                    C0 = 128 + cq * 4
                    P.op("vector", lambda e, ci=ci, cq=cq: e.tensor_tensor(
                        scr[:, 576 + cq * 8:576 + cq * 8 + 8], scr[:, 512 + ci * 8:512 + ci * 8 + 8],
                        ps[4][:, ci * 8:ci * 8 + 8], ALU.mult),
                         reads=[f"oh{ci}", PS[4]], writes=[f"psel{cq}"])
                    P.op("vector", lambda e, cq=cq, C0=C0: e.reduce_sum(
                        cols[:, C0:C0 + 1], scr[:, 576 + cq * 8:576 + cq * 8 + 8], axis=AX.X),
                         reads=[f"psel{cq}"], writes=[f"rc0_{cq}"])
                    P.op("vector", lambda e, eap=eids[:, col:col + 1], cq=cq, C0=C0, sub=sub: e.scalar_tensor_tensor(
                        cols[:, C0 + 1:C0 + 2], eap, float(CAP), cols[:, C0:C0 + 1], ALU.mult, ALU.add),
                         reads=[f"eids{sub % 4}", f"rc0_{cq}"], writes=[f"rc1_{cq}"])
                    P.op("vector", lambda e, cq=cq, C0=C0: e.tensor_scalar(
                        cols[:, C0 + 2:C0 + 3], cols[:, C0:C0 + 1], float(CAP), None, ALU.is_lt),
                         reads=[f"rc0_{cq}"], writes=[f"rc2_{cq}"])
                    P.op("vector", lambda e, cq=cq, C0=C0: e.tensor_scalar(
                        cols[:, C0 + 1:C0 + 2], cols[:, C0 + 1:C0 + 2], float(TRASH), None, ALU.subtract),
                         reads=[f"rc1_{cq}"], writes=[f"rc1_{cq}"])
                    P.op("vector", lambda e, cq=cq, C0=C0: e.tensor_tensor(
                        cols[:, C0 + 1:C0 + 2], cols[:, C0 + 1:C0 + 2], cols[:, C0 + 2:C0 + 3], ALU.mult),
                         reads=[f"rc1_{cq}", f"rc2_{cq}"], writes=[f"rc1_{cq}"])
                    P.op("vector", lambda e, cq=cq, C0=C0: e.tensor_scalar(
                        cols[:, C0 + 1:C0 + 2], cols[:, C0 + 1:C0 + 2], float(TRASH), None, ALU.add),
                         reads=[f"rc1_{cq}"], writes=[f"rc1_{cq}"])
                    P.op("vector", lambda e, col=col, cq=cq, C0=C0: e.tensor_copy(
                        gidxi[:, col:col + 1], cols[:, C0 + 1:C0 + 2]),
                         reads=[f"rc1_{cq}"], writes=["gidxi"])
                    P.op("vector", lambda e, b=b, r=r, sap=sap, sub=sub: e.tensor_scalar(
                        sxn[r][:], xrt[b][:], sap, None, ALU.mult),
                         reads=[f"xrt{b}", f"scl{sub % 4}"], writes=[f"sxn{r}"])
                    P.op("gpsimd", lambda e, r=r, col=col: e.indirect_dma_start(
                        xbkt, IOA(gidxi[:, col:col + 1], 0), sxn[r][:], None),
                         reads=["gidxi", f"sxn{r}"], writes=["xbkt"], dma=True)

            def emit_attn_block(n):
                tw = n % 2
                xn_t = xnb[tw]
                xn_r = [f"xnb{tw}_{dc}" for dc in range(8)]
                r0 = n * TB
                tokb = None
                for dc in range(8):
                    srcT = xnd[r0:r0 + TB, dc * 128:(dc + 1) * 128]
                    tokb = P.op("sync", lambda e, tw=tw, dc=dc, srcT=srcT: e.dma_start_transpose(
                        out=xnb[tw][dc][:], in_=srcT),
                         reads=[f"xnd{n}"], writes=[f"xnb{tw}_{dc}"], dma=True,
                         dma_sem=f"XNB{tw}")
                P.settle([f"xnb{tw}_{dc}" for dc in range(8)], tokb)
                for (wsb, wres, bias, biasres, dst, dname) in [
                        (wqsb, "Wq", bqc, "Cbq", qfm, "qfm"),
                        (wksb, "Wk", bkc, "Cbk", kfm, "kfm")]:
                    for oc in range(8):
                        pg, pgr = ps[oc % 4], PS[oc % 4]
                        for dc in range(8):
                            P.op("tensor", lambda e, pg=pg, wsb=wsb, oc=oc, dc=dc, xt=xn_t[dc]: e.matmul(
                                pg[:], wsb[:, dc * 1024 + oc * 128:dc * 1024 + (oc + 1) * 128],
                                xt[:], start=(dc == 0), stop=(dc == 7)),
                                 reads=[wres, xn_r[dc]], writes=[pgr])
                        P.op("scalar", lambda e, pg=pg, oc=oc, dst=dst, bias=bias: e.activation(
                            dst[oc][:], pg[:], AF.Sigmoid, bias=bias[:, oc:oc + 1]),
                             reads=[pgr, biasres], writes=[f"{dname}{oc}"])
                for sub in range(4):
                    for half in range(2):
                        pg, pgr = ps[2 + half], PS[2 + half]
                        for dc in range(8):
                            P.op("tensor", lambda e, pg=pg, sub=sub, half=half, dc=dc, xt=xn_t[dc]: e.matmul(
                                pg[:], xt[:, sub * 128:(sub + 1) * 128],
                                wvsb[:, dc * 1024 + half * 512:dc * 1024 + (half + 1) * 512],
                                start=(dc == 0), stop=(dc == 7)),
                                 reads=["Wv", xn_r[dc]], writes=[pgr])
                    for half in range(2):
                        pg, pgr = ps[2 + half], PS[2 + half]
                        P.op("scalar", lambda e, pg=pg, sub=sub, half=half: e.activation(
                            vtm[sub][:, half * 512:(half + 1) * 512], pg[:], AF.Copy, scale=1.0 - DECAY),
                             reads=[pgr], writes=[f"vtm{sub}"])
                        P.op("scalar", lambda e, pg=pg, sub=sub, half=half: e.activation(
                            vw[sub][:, half * 512:(half + 1) * 512], pg[:], AF.Copy, scale=cvw[:, sub:sub + 1]),
                             reads=[pgr, "Ccvw"], writes=[f"vw{sub}"])
                for sc in range(4):
                    pb = pbf if sc % 2 == 0 else pbf2
                    pbr = PBF if sc % 2 == 0 else PBF2
                    for dc in range(8):
                        P.op("tensor", lambda e, pb=pb, sc=sc, dc=dc: e.transpose(
                            pb[:, dc * 128:(dc + 1) * 128], kfm[dc][:, sc * 128:(sc + 1) * 128],
                            cidentb[:]),
                             reads=[f"kfm{dc}", "Cidb"], writes=[pbr])
                    P.op("vector", lambda e, pb=pb, sc=sc: e.tensor_copy(ktm[sc][:], pb[:]),
                         reads=[pbr], writes=[f"ktm{sc}"])
                for sc in range(4):
                    pg, pgr = ps[sc], PS[sc]
                    for dc in range(8):
                        P.op("tensor", lambda e, pg=pg, sc=sc, dc=dc: e.matmul(
                            pg[:], kfm[dc][:, sc * 128:(sc + 1) * 128], qfm[dc][:],
                            start=(dc == 0), stop=(dc == 7)),
                             reads=[f"kfm{dc}", f"qfm{dc}"], writes=[pgr])
                    P.op("vector", lambda e, pg=pg, sc=sc: e.tensor_tensor(
                        aTs[sc][:], pg[:], cdec[:, sc * 512:(sc + 1) * 512], ALU.mult),
                         reads=[pgr, "Ccdec"], writes=[f"aTs{sc}"])
                if n > 0:
                    for dc in range(8):
                        P.op("vector", lambda e, dc=dc: e.tensor_tensor(
                            qfm[dc][:], qfm[dc][:], cdrow[:], ALU.mult),
                             reads=[f"qfm{dc}", "Ccdrow"], writes=[f"qfm{dc}"])
                for dvc in range(8):
                    pg, pgr = ps[4 + dvc % 2], PS[4 + dvc % 2]
                    nmm = 4 + (8 if n > 0 else 0)
                    idx = 0
                    for sc in range(4):
                        st_, sp_ = (idx == 0), (idx == nmm - 1)
                        P.op("tensor", lambda e, pg=pg, dvc=dvc, sc=sc, st_=st_, sp_=sp_: e.matmul(
                            pg[:], vtm[sc][:, dvc * 128:(dvc + 1) * 128], aTs[sc][:],
                            start=st_, stop=sp_),
                             reads=[f"vtm{sc}", f"aTs{sc}"], writes=[pgr])
                        idx += 1
                    if n > 0:
                        for dc in range(8):
                            sp_ = (idx == nmm - 1)
                            P.op("tensor", lambda e, pg=pg, dvc=dvc, dc=dc, sp_=sp_: e.matmul(
                                pg[:], state[dc][:, dvc * 128:(dvc + 1) * 128], qfm[dc][:],
                                start=False, stop=sp_),
                                 reads=[f"state{dc}", f"qfm{dc}"], writes=[pgr])
                            idx += 1
                    P.op("scalar", lambda e, pg=pg, dvc=dvc: e.activation(yfm[dvc][:], pg[:], AF.Copy),
                         reads=[pgr], writes=[f"yfm{dvc}"])
                    ydst = yd[:, dvc * T + r0:dvc * T + r0 + TB]
                    P.op("scalar", lambda e, dvc=dvc, ydst=ydst: e.dma_start(out=ydst, in_=yfm[dvc][:]),
                         reads=[f"yfm{dvc}"], writes=["YD"], dma=True)
                for dc in range(8):
                    for half in range(2):
                        pg, pgr = ps[2 + half], PS[2 + half]
                        for sc in range(4):
                            P.op("tensor", lambda e, pg=pg, dc=dc, half=half, sc=sc: e.matmul(
                                pg[:], ktm[sc][:, dc * 128:(dc + 1) * 128],
                                vw[sc][:, half * 512:(half + 1) * 512],
                                start=(sc == 0), stop=(sc == 3)),
                                 reads=[f"ktm{sc}", f"vw{sc}"], writes=[pgr])
                    for half in range(2):
                        pg, pgr = ps[2 + half], PS[2 + half]
                        if n == 0:
                            P.op("scalar", lambda e, pg=pg, dc=dc, half=half: e.activation(
                                state[dc][:, half * 512:(half + 1) * 512], pg[:], AF.Copy),
                                 reads=[pgr], writes=[f"state{dc}"])
                        else:
                            kb = (dc * 2 + half) % 2
                            P.op("scalar", lambda e, pg=pg, kb=kb: e.activation(
                                kvst[kb][:], pg[:], AF.Copy),
                                 reads=[pgr], writes=[f"kvst{kb}"])
                            P.op("vector", lambda e, dc=dc, half=half, kb=kb: e.scalar_tensor_tensor(
                                state[dc][:, half * 512:(half + 1) * 512],
                                state[dc][:, half * 512:(half + 1) * 512], D512, kvst[kb][:],
                                ALU.mult, ALU.add),
                                 reads=[f"kvst{kb}", f"state{dc}"], writes=[f"state{dc}"])

            for sub in range(4):
                emit_ln_sub(sub)
            for tile in range(NTILE):
                for sub in range(4 * tile, 4 * tile + 4):
                    emit_ln_tr(sub)
                for sub in range(4 * tile, 4 * tile + 4):
                    emit_gate_sub(sub)
                    emit_route_oh(sub)
                if tile + 1 < NTILE:
                    for sub in range(4 * (tile + 1), 4 * (tile + 1) + 4):
                        emit_ln_sub(sub)
                emit_route_tile(tile)
            for tile in range(NTILE):
                emit_attn_block(tile)
            P.barrier()

        with ExitStack() as SB:
            EB = SB.enter_context
            xnfB = [[EB(nc.sbuf_tensor(f"xnfB{b}_{dc}", [128, 1024], BF16)) for dc in range(8)]
                    for b in range(2)]
            Xeg = [[EB(nc.sbuf_tensor(f"Xeg{b}_{dc}", [128, CAP], BF16)) for dc in range(8)]
                   for b in range(2)]
            hfm2 = [EB(nc.sbuf_tensor(f"hfm2_{hc}", [128, max(CAP, GC)], BF16)) for hc in range(8)]
            asil2 = EB(nc.sbuf_tensor("asil2", [128, max(CAP, GC)], BF16))
            w1sl2 = [EB(nc.sbuf_tensor(f"w1sl2{b}", [128, 1024], BF16)) for b in range(2)]
            w3sl2 = [EB(nc.sbuf_tensor(f"w3sl2{b}", [128, 1024], BF16)) for b in range(2)]
            w2sb2 = EB(nc.sbuf_tensor("w2sb2", [128, 8192], BF16))
            osp = [EB(nc.sbuf_tensor(f"osp{b}", [128, 1024], BF16)) for b in range(2)]
            xe0st = [EB(nc.sbuf_tensor(f"xe0st{b}", [128, 1024], BF16)) for b in range(2)]

            def emit_shared_group(grp):
                gb = grp % 2
                g0 = grp * GC
                tokg = None
                for dc in range(8):
                    srcT = xnd[g0:g0 + GC, dc * 128:(dc + 1) * 128]
                    tokg = P.op("sync", lambda e, gb=gb, dc=dc, srcT=srcT: e.dma_start_transpose(
                        out=xnfB[gb][dc][:], in_=srcT),
                         reads=[f"xnd{2 * grp}", f"xnd{2 * grp + 1}"],
                         writes=[f"xnfB{gb}_{dc}"], dma=True, dma_sem=f"XNB{gb}")
                P.settle([f"xnfB{gb}_{dc}" for dc in range(8)], tokg)
                for hc0 in range(2):
                    P.op("gpsimd", lambda e, hc0=hc0: e.dma_start(
                        out=w1sl2[hc0][:], in_=dram["w1s"][hc0]),
                         writes=[f"w1sl2{hc0}"], dma=True)
                    P.op("gpsimd", lambda e, hc0=hc0: e.dma_start(
                        out=w3sl2[hc0][:], in_=dram["w3s"][hc0]),
                         writes=[f"w3sl2{hc0}"], dma=True)
                P.op("gpsimd", lambda e: e.dma_start(out=w2sb2[:], in_=dram["w2s"]),
                     writes=["w2sb2"], dma=True)
                for hc in range(8):
                    wb = hc % 2
                    if hc >= 2:
                        P.op("gpsimd", lambda e, hc=hc, wb=wb: e.dma_start(
                            out=w1sl2[wb][:], in_=dram["w1s"][hc]),
                             writes=[f"w1sl2{wb}"], dma=True)
                        P.op("gpsimd", lambda e, hc=hc, wb=wb: e.dma_start(
                            out=w3sl2[wb][:], in_=dram["w3s"][hc]),
                             writes=[f"w3sl2{wb}"], dma=True)
                    for half in range(2):
                        for dc in range(8):
                            rap = xnfB[gb][dc][:, half * 512:(half + 1) * 512]
                            P.op("tensor", lambda e, dc=dc, wb=wb, half=half, rap=rap, gb=gb: e.matmul(
                                ps[half][:], w1sl2[wb][:, dc * 128:(dc + 1) * 128], rap,
                                start=(dc == 0), stop=(dc == 7)),
                                 reads=[f"w1sl2{wb}", f"xnfB{gb}_{dc}"], writes=[PS[half]])
                            P.op("tensor", lambda e, dc=dc, wb=wb, half=half, rap=rap, gb=gb: e.matmul(
                                ps[2 + half][:], w3sl2[wb][:, dc * 128:(dc + 1) * 128], rap,
                                start=(dc == 0), stop=(dc == 7)),
                                 reads=[f"w3sl2{wb}", f"xnfB{gb}_{dc}"], writes=[PS[2 + half]])
                        P.op("scalar", lambda e, half=half: e.activation(
                            asil2[:, half * 512:(half + 1) * 512], ps[half][:], AF.Silu),
                             reads=[PS[half]], writes=["asil2"])
                        P.op("vector", lambda e, hc=hc, half=half: e.tensor_tensor(
                            hfm2[hc][:, half * 512:(half + 1) * 512],
                            asil2[:, half * 512:(half + 1) * 512], ps[2 + half][:], ALU.mult),
                             reads=["asil2", PS[2 + half]], writes=[f"hfm2_{hc}"])
                for sub in range(8):
                    for half in range(2):
                        for hc in range(8):
                            P.op("tensor", lambda e, sub=sub, half=half, hc=hc: e.matmul(
                                ps[half][:], hfm2[hc][:, sub * 128:(sub + 1) * 128],
                                w2sb2[:, hc * 1024 + half * 512:hc * 1024 + (half + 1) * 512],
                                start=(hc == 0), stop=(hc == 7)),
                                 reads=[f"hfm2_{hc}", "w2sb2"], writes=[PS[half]])
                    b = sub % 2
                    for half in range(2):
                        P.op("scalar", lambda e, b=b, half=half: e.activation(
                            osp[b][:, half * 512:(half + 1) * 512], ps[half][:], AF.Copy),
                             reads=[PS[half]], writes=[f"osp{b}"])
                    srows = shd[g0 + sub * 128:g0 + (sub + 1) * 128, :]
                    P.op("scalar", lambda e, b=b, srows=srows: e.dma_start(out=srows, in_=osp[b][:]),
                         reads=[f"osp{b}"], writes=["shd"], dma=True)

            def emit_expert(e_i):
                eb = e_i % 2
                if e_i == 0:
                    for k in range(NCS):
                        kb = k % 2
                        srows = xbkt[k * 128:(k + 1) * 128, :]
                        P.op("sync", lambda e, kb=kb, srows=srows: e.dma_start(
                            out=xe0st[kb][:], in_=srows),
                             reads=["xbkt"], writes=[f"xe0st{kb}"], dma=True)
                        pb = pbf if k % 2 == 0 else pbf2
                        pbr = PBF if k % 2 == 0 else PBF2
                        for dc in range(8):
                            P.op("tensor", lambda e, pb=pb, kb=kb, dc=dc: e.transpose(
                                pb[:, dc * 128:(dc + 1) * 128],
                                xe0st[kb][:, dc * 128:(dc + 1) * 128], cidentb[:]),
                                 reads=[f"xe0st{kb}", "Cidb"], writes=[pbr])
                        for dc in range(8):
                            P.op("vector", lambda e, pb=pb, k=k, dc=dc: e.tensor_copy(
                                Xeg[0][dc][:, k * 128:(k + 1) * 128], pb[:, dc * 128:(dc + 1) * 128]),
                                 reads=[pbr], writes=[f"Xeg0_{dc}"])
                else:
                    tokx = None
                    for dc in range(8):
                        srcT = xbkt[e_i * CAP:(e_i + 1) * CAP, dc * 128:(dc + 1) * 128]
                        tokx = P.op("sync", lambda e, eb=eb, dc=dc, srcT=srcT: e.dma_start_transpose(
                            out=Xeg[eb][dc][:], in_=srcT),
                             reads=["xbkt"], writes=[f"Xeg{eb}_{dc}"], dma=True,
                             dma_sem=f"XEG{eb}")
                    P.settle([f"Xeg{eb}_{dc}" for dc in range(8)], tokx)
                for hc0 in range(2):
                    P.op("gpsimd", lambda e, e_i=e_i, hc0=hc0: e.dma_start(
                        out=w1sl2[hc0][:], in_=dram["w1"][e_i][hc0]),
                         writes=[f"w1sl2{hc0}"], dma=True)
                    P.op("gpsimd", lambda e, e_i=e_i, hc0=hc0: e.dma_start(
                        out=w3sl2[hc0][:], in_=dram["w3"][e_i][hc0]),
                         writes=[f"w3sl2{hc0}"], dma=True)
                P.op("gpsimd", lambda e, e_i=e_i: e.dma_start(out=w2sb2[:], in_=dram["w2"][e_i]),
                     writes=["w2sb2"], dma=True)
                for hc in range(8):
                    wb = hc % 2
                    if hc >= 2:
                        P.op("gpsimd", lambda e, e_i=e_i, hc=hc, wb=wb: e.dma_start(
                            out=w1sl2[wb][:], in_=dram["w1"][e_i][hc]),
                             writes=[f"w1sl2{wb}"], dma=True)
                        P.op("gpsimd", lambda e, e_i=e_i, hc=hc, wb=wb: e.dma_start(
                            out=w3sl2[wb][:], in_=dram["w3"][e_i][hc]),
                             writes=[f"w3sl2{wb}"], dma=True)
                    for si, (coff, cw) in enumerate(csplits):
                        for dc in range(8):
                            rap = Xeg[eb][dc][:, coff:coff + cw]
                            P.op("tensor", lambda e, dc=dc, wb=wb, si=si, cw=cw, rap=rap, eb=eb: e.matmul(
                                ps[si][:, 0:cw], w1sl2[wb][:, dc * 128:(dc + 1) * 128], rap,
                                start=(dc == 0), stop=(dc == 7)),
                                 reads=[f"w1sl2{wb}", f"Xeg{eb}_{dc}"], writes=[PS[si]])
                            P.op("tensor", lambda e, dc=dc, wb=wb, si=si, cw=cw, rap=rap, eb=eb: e.matmul(
                                ps[3 + si][:, 0:cw], w3sl2[wb][:, dc * 128:(dc + 1) * 128], rap,
                                start=(dc == 0), stop=(dc == 7)),
                                 reads=[f"w3sl2{wb}", f"Xeg{eb}_{dc}"], writes=[PS[3 + si]])
                        P.op("scalar", lambda e, si=si, coff=coff, cw=cw: e.activation(
                            asil2[:, coff:coff + cw], ps[si][:, 0:cw], AF.Silu),
                             reads=[PS[si]], writes=["asil2"])
                        P.op("vector", lambda e, hc=hc, si=si, coff=coff, cw=cw: e.tensor_tensor(
                            hfm2[hc][:, coff:coff + cw], asil2[:, coff:coff + cw],
                            ps[3 + si][:, 0:cw], ALU.mult),
                             reads=["asil2", PS[3 + si]], writes=[f"hfm2_{hc}"])
                for sub in range(NCS):
                    for half in range(2):
                        for hc in range(8):
                            P.op("tensor", lambda e, sub=sub, half=half, hc=hc: e.matmul(
                                ps[half][:], hfm2[hc][:, sub * 128:(sub + 1) * 128],
                                w2sb2[:, hc * 1024 + half * 512:hc * 1024 + (half + 1) * 512],
                                start=(hc == 0), stop=(hc == 7)),
                                 reads=[f"hfm2_{hc}", "w2sb2"], writes=[PS[half]])
                    b = sub % 2
                    for half in range(2):
                        P.op("scalar", lambda e, b=b, half=half: e.activation(
                            osp[b][:, half * 512:(half + 1) * 512], ps[half][:], AF.Copy),
                             reads=[PS[half]], writes=[f"osp{b}"])
                    orows = obkt[e_i * CAP + sub * 128:e_i * CAP + (sub + 1) * 128, :]
                    P.op("scalar", lambda e, b=b, orows=orows: e.dma_start(out=orows, in_=osp[b][:]),
                         reads=[f"osp{b}"], writes=["obkt"], dma=True)

            for grp in range(NGRP):
                emit_shared_group(grp)
            for e_i in range(NEXP):
                emit_expert(e_i)
            P.barrier()

        with ExitStack() as SC:
            EC = SC.enter_context
            wosb = EC(nc.sbuf_tensor("wosb", [128, 8192], BF16))
            P.op("gpsimd", lambda e: e.dma_start(out=wosb[:], in_=dram["wo"]),
                 writes=["Wo"], dma=True)
            ysl = [EC(nc.sbuf_tensor(f"ysl{b}", [128, 1024], BF16)) for b in range(2)]
            gq0 = [EC(nc.sbuf_tensor(f"gq0_{b}", [128, 1024], BF16)) for b in range(2)]
            gq1 = [EC(nc.sbuf_tensor(f"gq1_{b}", [128, 1024], BF16)) for b in range(2)]
            shsb = [EC(nc.sbuf_tensor(f"shsb{b}", [128, 1024], BF16)) for b in range(2)]
            fsum = [EC(nc.sbuf_tensor(f"fsum{b}", [128, 1024], BF16)) for b in range(2)]
            gcb = [EC(nc.sbuf_tensor(f"gcb{b}", [128, 1024], BF16)) for b in range(2)]
            xsb = [EC(nc.sbuf_tensor(f"xsb{b}", [128, 1024], F32)) for b in range(2)]
            osb = [EC(nc.sbuf_tensor(f"osb{b}", [128, 1024], F32)) for b in range(2)]

            for sub in range(NSUB):
                b2 = sub % 2
                ysrc = yd[:, sub * 128:].rearrange("p (dvc t) -> p dvc t", dvc=1) if False else None
                yap = yd[:, sub * 128:sub * 128 + (7 * T + 128)].rearrange(
                    "p (dvc rest) -> p dvc rest", dvc=1) if False else None
                P.op("sync", lambda e, b2=b2, sub=sub: e.dma_start(
                    out=ysl[b2][:],
                    in_=yd[:, :].rearrange("p (dvc t) -> p dvc t", dvc=8)[:, :, sub * 128:(sub + 1) * 128]),
                     reads=["YD"], writes=[f"ysl{b2}"], dma=True)
                P.op("gpsimd", lambda e, b2=b2, sub=sub: e.indirect_dma_start(
                    gq0[b2][:], None, obkt, IOA(gidxi[:, 2 * sub:2 * sub + 1], 0)),
                     reads=["gidxi", "obkt"], writes=[f"gq0_{b2}"], dma=True)
                P.op("gpsimd", lambda e, b2=b2, sub=sub: e.indirect_dma_start(
                    gq1[b2][:], None, obkt, IOA(gidxi[:, 2 * sub + 1:2 * sub + 2], 0)),
                     reads=["gidxi", "obkt"], writes=[f"gq1_{b2}"], dma=True)
                srows = shd[sub * 128:(sub + 1) * 128, :]
                P.op("sync", lambda e, b2=b2, srows=srows: e.dma_start(out=shsb[b2][:], in_=srows),
                     reads=["shd"], writes=[f"shsb{b2}"], dma=True)
                xrows = dram["x"][sub * 128:(sub + 1) * 128, :]
                P.op("sync", lambda e, b2=b2, xrows=xrows: e.dma_start(out=xsb[b2][:], in_=xrows),
                     writes=[f"xsb{b2}"], dma=True)
                P.op("vector", lambda e, b2=b2: e.tensor_tensor(
                    fsum[b2][:], gq0[b2][:], gq1[b2][:], ALU.add),
                     reads=[f"gq0_{b2}", f"gq1_{b2}"], writes=[f"fsum{b2}"])
                P.op("vector", lambda e, b2=b2: e.tensor_tensor(
                    fsum[b2][:], fsum[b2][:], shsb[b2][:], ALU.add),
                     reads=[f"shsb{b2}", f"fsum{b2}"], writes=[f"fsum{b2}"])
                P.op("scalar", lambda e, b2=b2: e.activation(gcb[b2][:], fsum[b2][:], AF.Silu),
                     reads=[f"fsum{b2}"], writes=[f"gcb{b2}"])
                for half in range(2):
                    pg, pgr = ps[half + 2 * (sub % 2)], PS[half + 2 * (sub % 2)]
                    for dvc in range(8):
                        P.op("tensor", lambda e, pg=pg, b2=b2, half=half, dvc=dvc: e.matmul(
                            pg[:], ysl[b2][:, dvc * 128:(dvc + 1) * 128],
                            wosb[:, dvc * 1024 + half * 512:dvc * 1024 + (half + 1) * 512],
                            start=(dvc == 0), stop=(dvc == 7)),
                             reads=[f"ysl{b2}", "Wo"], writes=[pgr])
                    P.op("vector", lambda e, pg=pg, b2=b2, half=half: e.tensor_tensor(
                        osb[b2][:, half * 512:(half + 1) * 512], pg[:],
                        gcb[b2][:, half * 512:(half + 1) * 512], ALU.mult),
                         reads=[pgr, f"gcb{b2}"], writes=[f"osb{b2}"])
                    P.op("vector", lambda e, b2=b2, half=half: e.tensor_tensor(
                        osb[b2][:, half * 512:(half + 1) * 512],
                        osb[b2][:, half * 512:(half + 1) * 512],
                        xsb[b2][:, half * 512:(half + 1) * 512], ALU.add),
                         reads=[f"xsb{b2}", f"osb{b2}"], writes=[f"osb{b2}"])
                orows = out_d[sub * 128:(sub + 1) * 128, :]
                P.op("scalar", lambda e, b2=b2, orows=orows: e.dma_start(out=orows, in_=osb[b2][:]),
                     reads=[f"osb{b2}"], writes=["OUT"], dma=True)
            P.barrier()
            P.emit(S0)
    return nc



_NC_CACHE = {}


def _get_nc(T):
    if T not in _NC_CACHE:
        _NC_CACHE[T] = build_nc(T)
    return _NC_CACHE[T]


def make_in_map(xb, ins):
    c = host_constants()
    return {
        "x": np.ascontiguousarray(np.asarray(xb, dtype=np.float32)),
        "wq": _lhsT_tiles(np.asarray(ins["L0_attn_Wq"], dtype=np.float32)),
        "wk": _lhsT_tiles(np.asarray(ins["L0_attn_Wk"], dtype=np.float32)),
        "wv": _rhs_tiles(np.asarray(ins["L0_attn_Wv"], dtype=np.float32)),
        "wo": _rhs_tiles(np.asarray(ins["L0_attn_Wo"], dtype=np.float32)),
        "bq": np.ascontiguousarray(np.asarray(ins["L0_attn_bq"], dtype=np.float32).reshape(8, 128).T),
        "bk": np.ascontiguousarray(np.asarray(ins["L0_attn_bk"], dtype=np.float32).reshape(8, 128).T),
        "gw": np.ascontiguousarray(
            np.asarray(ins["L0_ffn_gate_w"], dtype=np.float32).T.reshape(8, 128, 8).transpose(1, 0, 2).reshape(128, 64)),
        "w1": _w13_tiles(np.asarray(ins["L0_ffn_w1"], dtype=np.float32)),
        "w3": _w13_tiles(np.asarray(ins["L0_ffn_w3"], dtype=np.float32)),
        "w2": _w2_tiles(np.asarray(ins["L0_ffn_w2"], dtype=np.float32)),
        "w1s": _w13_tiles(np.asarray(ins["L0_ffn_w1s"], dtype=np.float32))[0],
        "w3s": _w13_tiles(np.asarray(ins["L0_ffn_w3s"], dtype=np.float32))[0],
        "w2s": _w2_tiles(np.asarray(ins["L0_ffn_w2s"], dtype=np.float32))[0],
        **c,
    }


def kernel(**inputs):
    x = np.asarray(inputs["L0_attn_x"], dtype=np.float32)
    B, T, E = x.shape
    nc = _get_nc(T)
    in_maps = [make_in_map(x[b], inputs) for b in range(B)]
    res = run_bass_kernel_spmd(nc, in_maps, core_ids=list(range(B)))
    out = np.stack([np.asarray(res.results[b]["out"]).reshape(T, E) for b in range(B)])
    return out.astype(np.float32)


# version 21 (final)
# speedup vs baseline: 1.1202x; 1.1202x over previous
import numpy as np
import ml_dtypes
from collections import defaultdict
from contextlib import ExitStack

import concourse.bass as bass
import concourse.mybir as mybir
from concourse.bass_utils import run_bass_kernel_spmd

F32 = mybir.dt.float32
BF16 = mybir.dt.bfloat16
I32 = mybir.dt.int32
AF = mybir.ActivationFunctionType
ALU = mybir.AluOpType
AX = mybir.AxisListType
IOA = bass.IndirectOffsetOnAxis

DECAY = 0.99
TB = 512
DM = 1024
NEXP = 8

ENGINES = ("sync", "tensor", "vector", "scalar", "gpsimd")


class Res:
    __slots__ = ("name", "writer", "readers", "dma_queue")

    def __init__(self, name):
        self.name = name
        self.writer = None
        self.readers = []
        self.dma_queue = None


class Prog:

    def __init__(self, nc):
        self.nc = nc
        self.ops = {e: [] for e in ENGINES}
        self.cnt = defaultdict(int)
        self.floors = defaultdict(dict)
        self.res = {}
        self.sems = {}

    def r(self, name):
        if name not in self.res:
            self.res[name] = Res(name)
        return self.res[name]

    def op(self, eng, emit, reads=(), writes=(), dma=False, dma_sem=None):
        reads = [self.r(x) if not isinstance(x, Res) else x for x in reads]
        writes = [self.r(x) if not isinstance(x, Res) else x for x in writes]
        waits = {}

        def add(tok):
            if tok is None:
                return
            sem, val = tok
            if eng == "tensor" and sem == "Stensor":
                return
            if self.floors[eng].get(sem, 0) >= val:
                return
            waits[sem] = max(waits.get(sem, 0), val)

        for rr in reads:
            add(rr.writer)
        for ww in writes:
            add(ww.writer)
            for rd in ww.readers:
                add(rd)
        for sem, val in waits.items():
            self.floors[eng][sem] = val
        if dma:
            assert len(writes) >= 1
            kind = "sw" if eng in ("gpsimd", "vector") else "hw"
            w0 = writes[0]
            if w0.dma_queue is None:
                w0.dma_queue = kind
            assert w0.dma_queue == kind, f"res {w0.name}: SWDGE/HWDGE sem conflict"
            sem_name, amt = "B" + (dma_sem if dma_sem else w0.name), 16
        else:
            sem_name, amt = "S" + eng, 1
        self.cnt[sem_name] += amt
        tok = (sem_name, self.cnt[sem_name])
        for ww in writes:
            ww.writer = tok
            ww.readers = []
        for rr in reads:
            rr.readers.append(tok)
        self.ops[eng].append((sorted(waits.items()), emit, sem_name, amt))
        return tok

    def settle(self, res_names, tok):
        for rn in res_names:
            self.r(rn).writer = tok

    def barrier(self):
        toks = [(s, c) for s, c in self.cnt.items() if c > 0]
        for e in ENGINES:
            waits = {}
            for sem, val in toks:
                if self.floors[e].get(sem, 0) >= val:
                    continue
                waits[sem] = max(waits.get(sem, 0), val)
            for sem, val in waits.items():
                self.floors[e][sem] = val
            if waits:
                self.ops[e].append((sorted(waits.items()), None, None, 0))

    def emit(self, exitstack):
        used = {s for s, c in self.cnt.items() if c > 0}
        for s in sorted(used):
            self.sems[s] = exitstack.enter_context(self.nc.semaphore(s))
        block = exitstack.enter_context(self.nc.Block())
        prog = self

        def section(eng_name):
            def body(eng):
                for waits, emit, sem_name, amt in prog.ops[eng_name]:
                    for wsem, wval in waits:
                        if wsem in prog.sems:
                            eng.wait_ge(prog.sems[wsem], wval)
                    if emit is not None:
                        inst = emit(eng)
                        if sem_name is not None:
                            inst.then_inc(prog.sems[sem_name], amt)
            return body

        for eng_name in ENGINES:
            if prog.ops[eng_name]:
                getattr(block, eng_name)(section(eng_name))
        return block



def _lhsT_tiles(w):
    return np.ascontiguousarray(
        w.reshape(8, 128, 8, 128).transpose(1, 0, 2, 3).reshape(128, 8192))


def _rhs_tiles(w):
    return np.ascontiguousarray(
        w.reshape(8, 128, 1024).transpose(1, 0, 2).reshape(128, 8192))


def _w13_tiles(w):
    return np.ascontiguousarray(
        w.reshape(-1, 8, 128, 8, 128).transpose(0, 3, 2, 1, 4).reshape(-1, 8, 128, 1024))


def _w2_tiles(w):
    return np.ascontiguousarray(
        w.reshape(-1, 8, 128, 1024).transpose(0, 2, 1, 3).reshape(-1, 128, 8192))


def host_constants():
    p = np.arange(128, dtype=np.float64)
    t512 = np.arange(TB, dtype=np.float64)
    cdec = np.zeros((128, 4 * TB), dtype=np.float32)
    for sc in range(4):
        s_abs = sc * 128 + p[:, None]
        m = t512[None, :] >= s_abs
        cdec[:, sc * TB:(sc + 1) * TB] = (m * DECAY ** (t512[None, :] - s_abs)).astype(np.float32)
    cdrow = np.ascontiguousarray(
        np.broadcast_to((DECAY ** t512).astype(np.float32), (128, TB)))
    cvw = np.zeros((128, 4), dtype=np.float32)
    for sub in range(4):
        cvw[:, sub] = (1.0 - DECAY) * DECAY ** (TB - (sub * 128 + p))
    ciota = np.ascontiguousarray(np.broadcast_to(np.arange(8, dtype=np.float32), (128, 8)))
    k = np.arange(128)
    cltri = (k[:, None] < k[None, :]).astype(np.float32)
    return {
        "cdec": cdec, "cdrow": cdrow, "cvw": cvw, "ciota": ciota,
        "cones": np.ones((1, 128), dtype=np.float32),
        "conesc": np.ones((128, 1), dtype=np.float32),
        "cident": np.eye(128, dtype=np.float32),
        "cltri": np.ascontiguousarray(cltri),
    }



def build_nc(T=4096):
    NBLK = T // TB
    NTILE = T // 512
    NSUB = T // 128
    NGRP = T // 1024
    GC = 1024
    D512 = float(DECAY ** TB)
    CAP = ((5 * T // 16 + 127) // 128) * 128
    TRASH = NEXP * CAP
    NCS = CAP // 128
    csplits = []
    off = 0
    while off < CAP:
        w = min(512, CAP - off)
        csplits.append((off, w))
        off += w

    nc = bass.Bass("TRN2", target_bir_lowering=False, debug=False)
    dram = {}
    for name, shape in [
        ("x", [T, DM]), ("wq", [128, 8192]), ("wk", [128, 8192]),
        ("wv", [128, 8192]), ("wo", [128, 8192]), ("bq", [128, 8]),
        ("bk", [128, 8]), ("gw", [128, 64]),
        ("w1", [NEXP, 8, 128, 1024]), ("w3", [NEXP, 8, 128, 1024]),
        ("w2", [NEXP, 128, 8192]),
        ("w1s", [8, 128, 1024]), ("w3s", [8, 128, 1024]), ("w2s", [128, 8192]),
        ("cdec", [128, 2048]), ("cdrow", [128, 512]), ("cvw", [128, 4]),
        ("ciota", [128, 8]), ("cones", [1, 128]), ("conesc", [128, 1]),
        ("cident", [128, 128]), ("cltri", [128, 128]),
    ]:
        dram[name] = nc.dram_tensor(name, shape, F32, kind="ExternalInput").ap()
    out_d = nc.dram_tensor("out", [T, DM], F32, kind="ExternalOutput").ap()
    xnd = nc.dram_tensor("xnd", [T, DM], BF16).ap()
    shd = nc.dram_tensor("shd", [T, DM], BF16).ap()
    xbkt = nc.dram_tensor("xbkt", [NEXP * CAP + 1, DM], BF16).ap()
    obkt = nc.dram_tensor("obkt", [NEXP * CAP + 1, DM], BF16).ap()

    with ExitStack() as S0:
        E0 = S0.enter_context
        P = Prog(nc)

        ps = [E0(nc.psum_tensor(f"ps{i}", [128, 512], F32)) for i in range(6)]
        pbf = E0(nc.psum_tensor("pbf", [128, 1024], BF16))
        pbf2 = E0(nc.psum_tensor("pbf2", [128, 1024], BF16))
        PS = [f"PS{i}" for i in range(6)]
        PBF = "PBF"
        PBF2 = "PBF2"

        cdec = E0(nc.sbuf_tensor("cdec_s", [128, 2048], F32))
        cdrow = E0(nc.sbuf_tensor("cdrow_s", [128, 512], F32))
        cvw = E0(nc.sbuf_tensor("cvw_s", [128, 4], F32))
        ciota = E0(nc.sbuf_tensor("ciota_s", [128, 8], F32))
        cones = E0(nc.sbuf_tensor("cones_s", [1, 128], F32))
        conesc = E0(nc.sbuf_tensor("conesc_s", [128, 1], F32))
        cltri = E0(nc.sbuf_tensor("cltri_s", [128, 128], F32))
        cidentb = E0(nc.sbuf_tensor("cidentb_s", [128, 128], BF16))
        bqc = E0(nc.sbuf_tensor("bqc_s", [128, 8], F32))
        bkc = E0(nc.sbuf_tensor("bkc_s", [128, 8], F32))
        for dn, tt, rn in [("cdec", cdec, "Ccdec"), ("cdrow", cdrow, "Ccdrow"),
                           ("cvw", cvw, "Ccvw"), ("ciota", ciota, "Cciota"),
                           ("cones", cones, "Ccones"), ("conesc", conesc, "Cconesc"),
                           ("cltri", cltri, "Ccltri"),
                           ("bq", bqc, "Cbq"), ("bk", bkc, "Cbk")]:
            P.op("sync", lambda e, tt=tt, dn=dn: e.dma_start(out=tt[:], in_=dram[dn]),
                 writes=[rn], dma=True)
        P.op("gpsimd", lambda e: e.dma_start(out=cidentb[:], in_=dram["cident"]),
             writes=["Cidb"], dma=True)
        gwsb = E0(nc.sbuf_tensor("gwsb", [128, 64], BF16))
        P.op("gpsimd", lambda e: e.dma_start(out=gwsb[:], in_=dram["gw"]),
             writes=["Cgw"], dma=True)
        cols = E0(nc.sbuf_tensor("cols", [128, 112], F32))
        scr = E0(nc.sbuf_tensor("scr", [128, 1024], F32))
        eids = E0(nc.sbuf_tensor("eids", [128, 2 * NSUB], F32))
        scl = E0(nc.sbuf_tensor("scl", [128, 2 * NSUB], F32))
        gidxi = E0(nc.sbuf_tensor("gidxi", [128, 2 * NSUB], I32))
        baserow = E0(nc.sbuf_tensor("baserow", [1, 16], F32))
        zrow = E0(nc.sbuf_tensor("zrow", [1, 1024], BF16))
        P.op("vector", lambda e: e.memset(baserow[0:1, 0:8], 0.0), writes=["baserow"])
        P.op("vector", lambda e: e.memset(zrow[:], 0.0), writes=["zrow"])
        P.op("scalar", lambda e: e.dma_start(out=obkt[TRASH:TRASH + 1, :], in_=zrow[:]),
             reads=["zrow"], writes=["obkt"], dma=True)

        with ExitStack() as S1:
            E1 = S1.enter_context
            xn_fm = [E1(nc.sbuf_tensor(f"xnfm{dc}", [128, T], BF16)) for dc in range(8)]

            with ExitStack() as S2:
                E2 = S2.enter_context
                xst = [E2(nc.sbuf_tensor(f"xst{b}", [128, 4096], F32)) for b in range(2)]
                sq = [E2(nc.sbuf_tensor(f"sq{i}", [128, 1024], F32)) for i in range(2)]
                xnt = [E2(nc.sbuf_tensor(f"xnt{b}", [128, 4096], BF16)) for b in range(2)]
                escr = E2(nc.sbuf_tensor("escr", [128, 16], F32))
                xrt = [E2(nc.sbuf_tensor(f"xrt{b}", [128, 1024], BF16)) for b in range(2)]
                sxn = [E2(nc.sbuf_tensor(f"sxn{b}", [128, 1024], BF16)) for b in range(2)]
                hfm = [E2(nc.sbuf_tensor(f"hfm{hc}", [128, GC], BF16)) for hc in range(8)]
                asil = E2(nc.sbuf_tensor("asil", [128, GC], BF16))
                w1sl = [E2(nc.sbuf_tensor(f"w1sl{b}", [128, 1024], BF16)) for b in range(2)]
                w3sl = [E2(nc.sbuf_tensor(f"w3sl{b}", [128, 1024], BF16)) for b in range(2)]
                w2sb = E2(nc.sbuf_tensor("w2sb", [128, 8192], BF16))
                gst = [E2(nc.sbuf_tensor(f"gst{b}", [128, 1024], BF16)) for b in range(2)]

                P.op("vector", lambda e: e.memset(w2sb[:], 0.0), writes=["w2sb"])
                r0z = 0
                while r0z < NEXP * CAP:
                    nr = min(1024, NEXP * CAP - r0z)
                    dst = xbkt[r0z:r0z + nr, :]
                    srcz = w2sb[:, 0:nr * 8]
                    P.op("gpsimd", lambda e, dst=dst, srcz=srcz: e.dma_start(out=dst, in_=srcz),
                         reads=["w2sb"], writes=["xbkt"], dma=True)
                    r0z += nr

                def emit_ln_tile(tile):
                    b = tile % 2
                    rows = dram["x"][tile * 512:(tile + 1) * 512, :].rearrange(
                        "(s p) d -> p s d", p=128)
                    P.op("sync", lambda e, b=b, rows=rows: e.dma_start(out=xst[b][:], in_=rows),
                         writes=[f"xst{b}"], dma=True)
                    def lncc(q):
                        return (lambda i: cols[:, 16 * q + i:16 * q + i + 1]), (lambda i: f"c{i}_{q}")
                    xres = f"xst{b}"
                    for sub in range(4):
                        q = sub % 4
                        cc, rn = lncc(q)
                        xsl = xst[b][:, sub * 1024:(sub + 1) * 1024]
                        P.op("vector", lambda e, xsl=xsl, cc=cc: e.reduce_sum(cc(0), xsl, axis=AX.X),
                             reads=[xres], writes=[rn(0)])
                        P.op("scalar", lambda e, xsl=xsl, cc=cc, q=q: e.activation(
                            sq[q % 2][:], xsl, AF.Square, accum_out=cc(1)),
                             reads=[xres], writes=[f"sq{q % 2}", rn(1)])
                    for sub in range(4):
                        q = sub % 4
                        cc, rn = lncc(q)
                        xsl = xst[b][:, sub * 1024:(sub + 1) * 1024]
                        P.op("vector", lambda e, cc=cc: e.tensor_scalar(cc(2), cc(0), 1.0 / DM, None, ALU.mult),
                             reads=[rn(0)], writes=[rn(2)])
                        P.op("vector", lambda e, cc=cc: e.tensor_scalar(cc(3), cc(1), 1.0 / DM, None, ALU.mult),
                             reads=[rn(1)], writes=[rn(3)])
                        P.op("vector", lambda e, cc=cc: e.tensor_tensor(cc(4), cc(2), cc(2), ALU.mult),
                             reads=[rn(2)], writes=[rn(4)])
                        P.op("vector", lambda e, cc=cc: e.tensor_tensor(cc(5), cc(3), cc(4), ALU.subtract),
                             reads=[rn(3), rn(4)], writes=[rn(5)])
                        P.op("vector", lambda e, cc=cc: e.tensor_scalar(cc(5), cc(5), 1e-5, None, ALU.add),
                             reads=[rn(5)], writes=[rn(5)])
                        P.op("scalar", lambda e, cc=cc: e.activation(cc(6), cc(5), AF.Sqrt),
                             reads=[rn(5)], writes=[rn(6)])
                        P.op("vector", lambda e, cc=cc: e.reciprocal(cc(7), cc(6)),
                             reads=[rn(6)], writes=[rn(7)])
                        P.op("vector", lambda e, cc=cc: e.tensor_scalar(
                            cc(8), cc(2), cc(7), -1.0, ALU.mult, ALU.mult),
                             reads=[rn(2), rn(7)], writes=[rn(8)])
                        P.op("scalar", lambda e, b=b, sub=sub, xsl=xsl, cc=cc: e.activation(
                            xnt[b][:, sub * 1024:(sub + 1) * 1024], xsl, AF.Identity,
                            bias=cc(8), scale=cc(7)),
                             reads=[xres, rn(7), rn(8)], writes=[f"xnt{b}"])
                    orows = xnd[tile * 512:(tile + 1) * 512, :].rearrange("(s p) d -> p s d", p=128)
                    P.op("scalar", lambda e, b=b, orows=orows: e.dma_start(out=orows, in_=xnt[b][:]),
                         reads=[f"xnt{b}"], writes=[f"xnd{tile}"], dma=True)
                    for sub in range(4):
                        pb = pbf if sub % 2 == 0 else pbf2
                        pbr = PBF if sub % 2 == 0 else PBF2
                        for dc in range(8):
                            insl = xnt[b][:, sub * 1024 + dc * 128:sub * 1024 + (dc + 1) * 128]
                            P.op("tensor", lambda e, pb=pb, dc=dc, insl=insl: e.transpose(
                                pb[:, dc * 128:(dc + 1) * 128], insl, cidentb[:]),
                                 reads=[f"xnt{b}", "Cidb"], writes=[pbr])
                        for dc in range(8):
                            dstap = xn_fm[dc][:, tile * 512 + sub * 128:tile * 512 + (sub + 1) * 128]
                            P.op("vector", lambda e, pb=pb, dc=dc, dstap=dstap: e.tensor_copy(
                                dstap, pb[:, dc * 128:(dc + 1) * 128]),
                                 reads=[pbr], writes=[f"xf{dc}_{tile}"])

                def emit_gate_sub(sub):
                    tile = sub // 4
                    q = sub % 2
                    B0 = 64 + 16 * q
                    G0 = 128 * q
                    pg, pgr = ps[sub % 2], PS[sub % 2]
                    s1ap = scl[:, 2 * sub:2 * sub + 1]
                    s2ap = scl[:, 2 * sub + 1:2 * sub + 2]
                    e1ap = eids[:, 2 * sub:2 * sub + 1]
                    e2ap = eids[:, 2 * sub + 1:2 * sub + 2]
                    cc = lambda i: cols[:, B0 + i:B0 + i + 1]
                    rn = lambda i: f"g{i}_{q}"
                    sl = lambda a, bnd: scr[:, G0 + a:G0 + bnd]
                    sres = lambda nm: f"{nm}_{q}"
                    for dc in range(8):
                        P.op("tensor", lambda e, pg=pg, sub=sub, dc=dc: e.matmul(
                            pg[:, 0:8], xn_fm[dc][:, sub * 128:(sub + 1) * 128],
                            gwsb[:, dc * 8:(dc + 1) * 8], start=(dc == 0), stop=(dc == 7)),
                             reads=[f"xf{dc}_{tile}", "Cgw"], writes=[pgr])
                    P.op("vector", lambda e, pg=pg, cc=cc: e.reduce_max(cc(8), pg[:, 0:8], axis=AX.X, negate=True),
                         reads=[pgr], writes=[rn(8)])
                    P.op("scalar", lambda e, pg=pg, cc=cc, q=q: e.activation(escr[:, 8 * q:8 * q + 8], pg[:, 0:8], AF.Exp, bias=cc(8)),
                         reads=[pgr, rn(8)], writes=[sres("escr")])
                    P.op("vector", lambda e, cc=cc, q=q: e.reduce_sum(cc(9), escr[:, 8 * q:8 * q + 8], axis=AX.X),
                         reads=[sres("escr")], writes=[rn(9)])
                    P.op("vector", lambda e, cc=cc: e.reciprocal(cc(10), cc(9)),
                         reads=[rn(9)], writes=[rn(10)])
                    P.op("vector", lambda e, cc=cc, q=q, sl=sl: e.tensor_scalar(sl(0, 8), escr[:, 8 * q:8 * q + 8], cc(10), None, ALU.mult),
                         reads=[sres("escr"), rn(10)], writes=[sres("sc8")])
                    P.op("vector", lambda e, s1ap=s1ap, sl=sl: e.reduce_max(s1ap, sl(0, 8), axis=AX.X),
                         reads=[sres("sc8")], writes=[f"scl{q}"])
                    P.op("vector", lambda e, s1ap=s1ap, sl=sl: e.tensor_scalar(sl(8, 16), sl(0, 8), s1ap, None, ALU.is_equal),
                         reads=[sres("sc8"), f"scl{q}"], writes=[sres("m1")])
                    P.op("vector", lambda e, sl=sl: e.scalar_tensor_tensor(sl(16, 24), sl(8, 16), -8.0, ciota[:], ALU.mult, ALU.add),
                         reads=[sres("m1"), "Cciota"], writes=[sres("cand")])
                    P.op("vector", lambda e, cc=cc, sl=sl: e.tensor_reduce(cc(12), sl(16, 24), AX.X, ALU.min),
                         reads=[sres("cand")], writes=[rn(12)])
                    P.op("vector", lambda e, e1ap=e1ap, cc=cc: e.tensor_scalar(e1ap, cc(12), 8.0, None, ALU.add),
                         reads=[rn(12)], writes=[f"eids{q}"])
                    P.op("vector", lambda e, e1ap=e1ap, sl=sl: e.tensor_scalar(sl(8, 16), ciota[:], e1ap, None, ALU.is_equal),
                         reads=["Cciota", f"eids{q}"], writes=[sres("m1")])
                    P.op("vector", lambda e, sl=sl: e.scalar_tensor_tensor(sl(16, 24), sl(8, 16), -1e30, sl(0, 8), ALU.mult, ALU.add),
                         reads=[sres("m1"), sres("sc8")], writes=[sres("cand")])
                    P.op("vector", lambda e, s2ap=s2ap, sl=sl: e.reduce_max(s2ap, sl(16, 24), axis=AX.X),
                         reads=[sres("cand")], writes=[f"scl{q}"])
                    P.op("vector", lambda e, s2ap=s2ap, sl=sl: e.tensor_scalar(sl(24, 32), sl(16, 24), s2ap, None, ALU.is_equal),
                         reads=[sres("cand"), f"scl{q}"], writes=[sres("m2")])
                    P.op("vector", lambda e, sl=sl: e.scalar_tensor_tensor(sl(24, 32), sl(24, 32), -8.0, ciota[:], ALU.mult, ALU.add),
                         reads=[sres("m2"), "Cciota"], writes=[sres("m2")])
                    P.op("vector", lambda e, cc=cc, sl=sl: e.tensor_reduce(cc(14), sl(24, 32), AX.X, ALU.min),
                         reads=[sres("m2")], writes=[rn(14)])
                    P.op("vector", lambda e, e2ap=e2ap, cc=cc: e.tensor_scalar(e2ap, cc(14), 8.0, None, ALU.add),
                         reads=[rn(14)], writes=[f"eids{q}"])

                def emit_route_sub(sub):
                    tile = sub // 4
                    b = sub % 2
                    xrows = xnd[sub * 128:(sub + 1) * 128, :]
                    P.op("sync", lambda e, b=b, xrows=xrows: e.dma_start(out=xrt[b][:], in_=xrows),
                         reads=[f"xnd{tile}"], writes=[f"xrt{b}"], dma=True)
                    for r in range(2):
                        col = 2 * sub + r
                        eap = eids[:, col:col + 1]
                        sap = scl[:, col:col + 1]
                        P.op("vector", lambda e, eap=eap: e.tensor_scalar(
                            scr[:, 32:40], ciota[:], eap, None, ALU.is_equal),
                             reads=["Cciota", f"eids{sub % 2}"], writes=["oh"])
                        P.op("tensor", lambda e: e.matmul(
                            ps[2][:, 0:8], cltri[:], scr[:, 32:40], start=True, stop=False),
                             reads=["Ccltri", "oh"], writes=[PS[2]])
                        P.op("tensor", lambda e: e.matmul(
                            ps[2][:, 0:8], cones[:], baserow[0:1, 0:8], start=False, stop=True),
                             reads=["Ccones", "baserow"], writes=[PS[2]])
                        P.op("tensor", lambda e: e.matmul(
                            ps[3][0:1, 0:8], conesc[:], scr[:, 32:40], start=True, stop=True),
                             reads=["Cconesc", "oh"], writes=[PS[3]])
                        P.op("vector", lambda e: e.tensor_tensor(
                            scr[:, 40:48], scr[:, 32:40], ps[2][:, 0:8], ALU.mult),
                             reads=["oh", PS[2]], writes=["psel"])
                        P.op("vector", lambda e: e.reduce_sum(cols[:, 96:97], scr[:, 40:48], axis=AX.X),
                             reads=["psel"], writes=["c0"])
                        P.op("vector", lambda e: e.tensor_tensor(
                            baserow[0:1, 0:8], baserow[0:1, 0:8], ps[3][0:1, 0:8], ALU.add),
                             reads=[PS[3], "baserow"], writes=["baserow"])
                        P.op("vector", lambda e, eap=eap: e.scalar_tensor_tensor(
                            cols[:, 97:98], eap, float(CAP), cols[:, 96:97], ALU.mult, ALU.add),
                             reads=[f"eids{sub % 2}", "c0"], writes=["c1"])
                        P.op("vector", lambda e: e.tensor_scalar(cols[:, 98:99], cols[:, 96:97], float(CAP), None, ALU.is_lt),
                             reads=["c0"], writes=["c2"])
                        P.op("vector", lambda e: e.tensor_scalar(cols[:, 97:98], cols[:, 97:98], float(TRASH), None, ALU.subtract),
                             reads=["c1"], writes=["c1"])
                        P.op("vector", lambda e: e.tensor_tensor(cols[:, 97:98], cols[:, 97:98], cols[:, 98:99], ALU.mult),
                             reads=["c1", "c2"], writes=["c1"])
                        P.op("vector", lambda e: e.tensor_scalar(cols[:, 97:98], cols[:, 97:98], float(TRASH), None, ALU.add),
                             reads=["c1"], writes=["c1"])
                        P.op("vector", lambda e, col=col: e.tensor_copy(gidxi[:, col:col + 1], cols[:, 97:98]),
                             reads=["c1"], writes=["gidxi"])
                        P.op("vector", lambda e, b=b, r=r, sap=sap: e.tensor_scalar(
                            sxn[r][:], xrt[b][:], sap, None, ALU.mult),
                             reads=[f"xrt{b}", f"scl{sub % 2}"], writes=[f"sxn{r}"])
                        P.op("gpsimd", lambda e, r=r, col=col: e.indirect_dma_start(
                            xbkt, IOA(gidxi[:, col:col + 1], 0), sxn[r][:], None),
                             reads=["gidxi", f"sxn{r}"], writes=["xbkt"], dma=True)

                def emit_shared_loads(grp):
                    for hc0 in range(2):
                        P.op("gpsimd", lambda e, hc0=hc0: e.dma_start(
                            out=w1sl[hc0][:], in_=dram["w1s"][hc0]),
                             writes=[f"w1sl{hc0}"], dma=True)
                        P.op("gpsimd", lambda e, hc0=hc0: e.dma_start(
                            out=w3sl[hc0][:], in_=dram["w3s"][hc0]),
                             writes=[f"w3sl{hc0}"], dma=True)
                    P.op("gpsimd", lambda e: e.dma_start(out=w2sb[:], in_=dram["w2s"]),
                         writes=["w2sb"], dma=True)

                def emit_shared_group(grp):
                    g0 = grp * GC
                    for hc in range(8):
                        wb = hc % 2
                        if hc >= 2:
                            P.op("gpsimd", lambda e, hc=hc, wb=wb: e.dma_start(
                                out=w1sl[wb][:], in_=dram["w1s"][hc]),
                                 writes=[f"w1sl{wb}"], dma=True)
                            P.op("gpsimd", lambda e, hc=hc, wb=wb: e.dma_start(
                                out=w3sl[wb][:], in_=dram["w3s"][hc]),
                                 writes=[f"w3sl{wb}"], dma=True)
                        for half in range(2):
                            tile = 2 * grp + half
                            for dc in range(8):
                                rap = xn_fm[dc][:, g0 + half * 512:g0 + (half + 1) * 512]
                                P.op("tensor", lambda e, dc=dc, wb=wb, half=half, rap=rap: e.matmul(
                                    ps[half][:], w1sl[wb][:, dc * 128:(dc + 1) * 128], rap,
                                    start=(dc == 0), stop=(dc == 7)),
                                     reads=[f"w1sl{wb}", f"xf{dc}_{tile}"], writes=[PS[half]])
                                P.op("tensor", lambda e, dc=dc, wb=wb, half=half, rap=rap: e.matmul(
                                    ps[2 + half][:], w3sl[wb][:, dc * 128:(dc + 1) * 128], rap,
                                    start=(dc == 0), stop=(dc == 7)),
                                     reads=[f"w3sl{wb}", f"xf{dc}_{tile}"], writes=[PS[2 + half]])
                            P.op("scalar", lambda e, half=half: e.activation(
                                asil[:, half * 512:(half + 1) * 512], ps[half][:], AF.Silu),
                                 reads=[PS[half]], writes=["asil"])
                            P.op("vector", lambda e, hc=hc, half=half: e.tensor_tensor(
                                hfm[hc][:, half * 512:(half + 1) * 512],
                                asil[:, half * 512:(half + 1) * 512], ps[2 + half][:], ALU.mult),
                                 reads=["asil", PS[2 + half]], writes=[f"hfm{hc}"])
                    for sub in range(8):
                        for half in range(2):
                            for hc in range(8):
                                P.op("tensor", lambda e, sub=sub, half=half, hc=hc: e.matmul(
                                    ps[half][:], hfm[hc][:, sub * 128:(sub + 1) * 128],
                                    w2sb[:, hc * 1024 + half * 512:hc * 1024 + (half + 1) * 512],
                                    start=(hc == 0), stop=(hc == 7)),
                                     reads=[f"hfm{hc}", "w2sb"], writes=[PS[half]])
                        b = sub % 2
                        for half in range(2):
                            P.op("scalar", lambda e, b=b, half=half: e.activation(
                                gst[b][:, half * 512:(half + 1) * 512], ps[half][:], AF.Copy),
                                 reads=[PS[half]], writes=[f"gst{b}"])
                        srows = shd[g0 + sub * 128:g0 + (sub + 1) * 128, :]
                        P.op("scalar", lambda e, b=b, srows=srows: e.dma_start(out=srows, in_=gst[b][:]),
                             reads=[f"gst{b}"], writes=["shd"], dma=True)

                for tile in range(NTILE):
                    emit_ln_tile(tile)
                for sub in range(NSUB):
                    emit_gate_sub(sub)
                for g in range(NGRP):
                    emit_shared_loads(g)
                    emit_shared_group(g)
                    for sub in range(8 * g, 8 * g + 8):
                        emit_route_sub(sub)
                P.barrier()

        S45 = S0.enter_context(ExitStack())
        E45 = S45.enter_context
        wqsb = E45(nc.sbuf_tensor("wqsb", [128, 8192], BF16))
        wksb = E45(nc.sbuf_tensor("wksb", [128, 8192], BF16))
        wvsb = E45(nc.sbuf_tensor("wvsb", [128, 8192], BF16))
        wosb = E45(nc.sbuf_tensor("wosb", [128, 8192], BF16))
        for nm, tt, rn in [("wq", wqsb, "Wq"), ("wk", wksb, "Wk"),
                           ("wv", wvsb, "Wv"), ("wo", wosb, "Wo")]:
            P.op("gpsimd", lambda e, tt=tt, nm=nm: e.dma_start(out=tt[:], in_=dram[nm]),
                 writes=[rn], dma=True)
        with ExitStack() as S5:
            E5 = S5.enter_context
            Xeg = [[E5(nc.sbuf_tensor(f"Xeg{b}_{dc}", [128, CAP], BF16)) for dc in range(8)]
                   for b in range(2)]
            hfm2 = [E5(nc.sbuf_tensor(f"hfm2_{hc}", [128, CAP], BF16)) for hc in range(8)]
            asil2 = E5(nc.sbuf_tensor("asil2", [128, CAP], BF16))
            w1sl2 = [E5(nc.sbuf_tensor(f"w1sl2{b}", [128, 1024], BF16)) for b in range(2)]
            w3sl2 = [E5(nc.sbuf_tensor(f"w3sl2{b}", [128, 1024], BF16)) for b in range(2)]
            w2sb2 = E5(nc.sbuf_tensor("w2sb2", [128, 8192], BF16))
            osp = [E5(nc.sbuf_tensor(f"osp{b}", [128, 1024], BF16)) for b in range(2)]

            xe0st = [E5(nc.sbuf_tensor(f"xe0st{b}", [128, 1024], BF16)) for b in range(2)]
            for e_i in range(NEXP):
                eb = e_i % 2
                if e_i == 0:
                    for k in range(NCS):
                        kb = k % 2
                        srows = xbkt[k * 128:(k + 1) * 128, :]
                        P.op("sync", lambda e, kb=kb, srows=srows: e.dma_start(
                            out=xe0st[kb][:], in_=srows),
                             reads=["xbkt"], writes=[f"xe0st{kb}"], dma=True)
                        pb = pbf if k % 2 == 0 else pbf2
                        pbr = PBF if k % 2 == 0 else PBF2
                        for dc in range(8):
                            P.op("tensor", lambda e, pb=pb, kb=kb, dc=dc: e.transpose(
                                pb[:, dc * 128:(dc + 1) * 128],
                                xe0st[kb][:, dc * 128:(dc + 1) * 128], cidentb[:]),
                                 reads=[f"xe0st{kb}", "Cidb"], writes=[pbr])
                        for dc in range(8):
                            P.op("vector", lambda e, pb=pb, k=k, dc=dc: e.tensor_copy(
                                Xeg[0][dc][:, k * 128:(k + 1) * 128], pb[:, dc * 128:(dc + 1) * 128]),
                                 reads=[pbr], writes=[f"Xeg0_{dc}"])
                else:
                    tokx = None
                    for dc in range(8):
                        src = xbkt[e_i * CAP:(e_i + 1) * CAP, dc * 128:(dc + 1) * 128]
                        tokx = P.op("sync", lambda e, eb=eb, dc=dc, src=src: e.dma_start_transpose(
                            out=Xeg[eb][dc][:], in_=src),
                             reads=["xbkt"], writes=[f"Xeg{eb}_{dc}"], dma=True,
                             dma_sem=f"XEG{eb}")
                    P.settle([f"Xeg{eb}_{dc}" for dc in range(8)], tokx)
                for hc0 in range(2):
                    P.op("gpsimd", lambda e, e_i=e_i, hc0=hc0: e.dma_start(
                        out=w1sl2[hc0][:], in_=dram["w1"][e_i][hc0]),
                         writes=[f"w1sl2{hc0}"], dma=True)
                    P.op("gpsimd", lambda e, e_i=e_i, hc0=hc0: e.dma_start(
                        out=w3sl2[hc0][:], in_=dram["w3"][e_i][hc0]),
                         writes=[f"w3sl2{hc0}"], dma=True)
                P.op("gpsimd", lambda e, e_i=e_i: e.dma_start(out=w2sb2[:], in_=dram["w2"][e_i]),
                     writes=["w2sb2"], dma=True)
                for hc in range(8):
                    wb = hc % 2
                    if hc >= 2:
                        P.op("gpsimd", lambda e, e_i=e_i, hc=hc, wb=wb: e.dma_start(
                            out=w1sl2[wb][:], in_=dram["w1"][e_i][hc]),
                             writes=[f"w1sl2{wb}"], dma=True)
                        P.op("gpsimd", lambda e, e_i=e_i, hc=hc, wb=wb: e.dma_start(
                            out=w3sl2[wb][:], in_=dram["w3"][e_i][hc]),
                             writes=[f"w3sl2{wb}"], dma=True)
                    for si, (coff, cw) in enumerate(csplits):
                        for dc in range(8):
                            rap = Xeg[eb][dc][:, coff:coff + cw]
                            P.op("tensor", lambda e, dc=dc, wb=wb, si=si, cw=cw, rap=rap, eb=eb: e.matmul(
                                ps[si][:, 0:cw], w1sl2[wb][:, dc * 128:(dc + 1) * 128], rap,
                                start=(dc == 0), stop=(dc == 7)),
                                 reads=[f"w1sl2{wb}", f"Xeg{eb}_{dc}"], writes=[PS[si]])
                            P.op("tensor", lambda e, dc=dc, wb=wb, si=si, cw=cw, rap=rap, eb=eb: e.matmul(
                                ps[3 + si][:, 0:cw], w3sl2[wb][:, dc * 128:(dc + 1) * 128], rap,
                                start=(dc == 0), stop=(dc == 7)),
                                 reads=[f"w3sl2{wb}", f"Xeg{eb}_{dc}"], writes=[PS[3 + si]])
                        P.op("scalar", lambda e, si=si, coff=coff, cw=cw: e.activation(
                            asil2[:, coff:coff + cw], ps[si][:, 0:cw], AF.Silu),
                             reads=[PS[si]], writes=["asil2"])
                        P.op("vector", lambda e, hc=hc, si=si, coff=coff, cw=cw: e.tensor_tensor(
                            hfm2[hc][:, coff:coff + cw], asil2[:, coff:coff + cw],
                            ps[3 + si][:, 0:cw], ALU.mult),
                             reads=["asil2", PS[3 + si]], writes=[f"hfm2_{hc}"])
                for sub in range(NCS):
                    for half in range(2):
                        for hc in range(8):
                            P.op("tensor", lambda e, sub=sub, half=half, hc=hc: e.matmul(
                                ps[half][:], hfm2[hc][:, sub * 128:(sub + 1) * 128],
                                w2sb2[:, hc * 1024 + half * 512:hc * 1024 + (half + 1) * 512],
                                start=(hc == 0), stop=(hc == 7)),
                                 reads=[f"hfm2_{hc}", "w2sb2"], writes=[PS[half]])
                    b = sub % 2
                    for half in range(2):
                        P.op("scalar", lambda e, b=b, half=half: e.activation(
                            osp[b][:, half * 512:(half + 1) * 512], ps[half][:], AF.Copy),
                             reads=[PS[half]], writes=[f"osp{b}"])
                    orows = obkt[e_i * CAP + sub * 128:e_i * CAP + (sub + 1) * 128, :]
                    P.op("scalar", lambda e, b=b, orows=orows: e.dma_start(out=orows, in_=osp[b][:]),
                         reads=[f"osp{b}"], writes=["obkt"], dma=True)
            P.barrier()

        with ExitStack() as S4:
            E4 = S4.enter_context
            xnb = [[E4(nc.sbuf_tensor(f"xnb{b}_{dc}", [128, 512], BF16)) for dc in range(8)]
                   for b in range(2)]
            qfm = [E4(nc.sbuf_tensor(f"qfm{dc}", [128, 512], BF16)) for dc in range(8)]
            kfm = [E4(nc.sbuf_tensor(f"kfm{dc}", [128, 512], BF16)) for dc in range(8)]
            vtm = [E4(nc.sbuf_tensor(f"vtm{s}", [128, 1024], BF16)) for s in range(4)]
            vw = [E4(nc.sbuf_tensor(f"vw{s}", [128, 1024], BF16)) for s in range(4)]
            ktm = [E4(nc.sbuf_tensor(f"ktm{s}", [128, 1024], BF16)) for s in range(4)]
            aTs = [E4(nc.sbuf_tensor(f"aTs{s}", [128, 512], BF16)) for s in range(4)]
            yfm = [E4(nc.sbuf_tensor(f"yfm{dc}", [128, 512], BF16)) for dc in range(8)]
            state = [E4(nc.sbuf_tensor(f"state{dc}", [128, 1024], BF16)) for dc in range(8)]
            kvst = [E4(nc.sbuf_tensor(f"kvst{b}", [128, 512], BF16)) for b in range(2)]
            gq0 = [E4(nc.sbuf_tensor(f"gq0_{b}", [128, 1024], BF16)) for b in range(2)]
            gq1 = [E4(nc.sbuf_tensor(f"gq1_{b}", [128, 1024], BF16)) for b in range(2)]
            shsb = [E4(nc.sbuf_tensor(f"shsb{b}", [128, 1024], BF16)) for b in range(2)]
            fsum = [E4(nc.sbuf_tensor(f"fsum{b}", [128, 1024], BF16)) for b in range(2)]
            gcb = [E4(nc.sbuf_tensor(f"gcb{b}", [128, 1024], BF16)) for b in range(2)]
            xsb = [E4(nc.sbuf_tensor(f"xsb{b}", [128, 1024], F32)) for b in range(2)]
            osb = [E4(nc.sbuf_tensor(f"osb{b}", [128, 1024], F32)) for b in range(2)]

            for n in range(NBLK):
                bb = n % 2
                r0 = n * TB
                tokb = None
                for dc in range(8):
                    src = xnd[r0:r0 + TB, dc * 128:(dc + 1) * 128]
                    tokb = P.op("sync", lambda e, bb=bb, dc=dc, src=src: e.dma_start_transpose(
                        out=xnb[bb][dc][:], in_=src),
                         reads=[f"xnd{n}"], writes=[f"xnb{bb}_{dc}"], dma=True,
                         dma_sem=f"XNB{bb}")
                P.settle([f"xnb{bb}_{dc}" for dc in range(8)], tokb)
                xn_r = [f"xnb{bb}_{dc}" for dc in range(8)]
                xn_t = xnb[bb]
                for (wsb, wres, bias, biasres, dst, dname) in [
                        (wqsb, "Wq", bqc, "Cbq", qfm, "qfm"),
                        (wksb, "Wk", bkc, "Cbk", kfm, "kfm")]:
                    for oc in range(8):
                        pg, pgr = ps[oc % 4], PS[oc % 4]
                        for dc in range(8):
                            P.op("tensor", lambda e, pg=pg, wsb=wsb, oc=oc, dc=dc, xt=xn_t[dc]: e.matmul(
                                pg[:], wsb[:, dc * 1024 + oc * 128:dc * 1024 + (oc + 1) * 128],
                                xt[:], start=(dc == 0), stop=(dc == 7)),
                                 reads=[wres, xn_r[dc]], writes=[pgr])
                        P.op("scalar", lambda e, pg=pg, oc=oc, dst=dst, bias=bias: e.activation(
                            dst[oc][:], pg[:], AF.Sigmoid, bias=bias[:, oc:oc + 1]),
                             reads=[pgr, biasres], writes=[f"{dname}{oc}"])
                for sub in range(4):
                    for half in range(2):
                        pg, pgr = ps[2 + half], PS[2 + half]
                        for dc in range(8):
                            P.op("tensor", lambda e, pg=pg, sub=sub, half=half, dc=dc, xt=xn_t[dc]: e.matmul(
                                pg[:], xt[:, sub * 128:(sub + 1) * 128],
                                wvsb[:, dc * 1024 + half * 512:dc * 1024 + (half + 1) * 512],
                                start=(dc == 0), stop=(dc == 7)),
                                 reads=["Wv", xn_r[dc]], writes=[pgr])
                    for half in range(2):
                        pg, pgr = ps[2 + half], PS[2 + half]
                        P.op("scalar", lambda e, pg=pg, sub=sub, half=half: e.activation(
                            vtm[sub][:, half * 512:(half + 1) * 512], pg[:], AF.Copy, scale=1.0 - DECAY),
                             reads=[pgr], writes=[f"vtm{sub}"])
                        P.op("scalar", lambda e, pg=pg, sub=sub, half=half: e.activation(
                            vw[sub][:, half * 512:(half + 1) * 512], pg[:], AF.Copy, scale=cvw[:, sub:sub + 1]),
                             reads=[pgr, "Ccvw"], writes=[f"vw{sub}"])
                for sc in range(4):
                    pb = pbf if sc % 2 == 0 else pbf2
                    pbr = PBF if sc % 2 == 0 else PBF2
                    for dc in range(8):
                        P.op("tensor", lambda e, pb=pb, sc=sc, dc=dc: e.transpose(
                            pb[:, dc * 128:(dc + 1) * 128], kfm[dc][:, sc * 128:(sc + 1) * 128],
                            cidentb[:]),
                             reads=[f"kfm{dc}", "Cidb"], writes=[pbr])
                    P.op("vector", lambda e, pb=pb, sc=sc: e.tensor_copy(ktm[sc][:], pb[:]),
                         reads=[pbr], writes=[f"ktm{sc}"])
                for sc in range(4):
                    pg, pgr = ps[sc], PS[sc]
                    for dc in range(8):
                        P.op("tensor", lambda e, pg=pg, sc=sc, dc=dc: e.matmul(
                            pg[:], kfm[dc][:, sc * 128:(sc + 1) * 128], qfm[dc][:],
                            start=(dc == 0), stop=(dc == 7)),
                             reads=[f"kfm{dc}", f"qfm{dc}"], writes=[pgr])
                    P.op("vector", lambda e, pg=pg, sc=sc: e.tensor_tensor(
                        aTs[sc][:], pg[:], cdec[:, sc * 512:(sc + 1) * 512], ALU.mult),
                         reads=[pgr, "Ccdec"], writes=[f"aTs{sc}"])
                if n > 0:
                    for dc in range(8):
                        P.op("vector", lambda e, dc=dc: e.tensor_tensor(
                            qfm[dc][:], qfm[dc][:], cdrow[:], ALU.mult),
                             reads=[f"qfm{dc}", "Ccdrow"], writes=[f"qfm{dc}"])
                for dvc in range(8):
                    pg, pgr = ps[4 + dvc % 2], PS[4 + dvc % 2]
                    nmm = 4 + (8 if n > 0 else 0)
                    idx = 0
                    for sc in range(4):
                        st_, sp_ = (idx == 0), (idx == nmm - 1)
                        P.op("tensor", lambda e, pg=pg, dvc=dvc, sc=sc, st_=st_, sp_=sp_: e.matmul(
                            pg[:], vtm[sc][:, dvc * 128:(dvc + 1) * 128], aTs[sc][:],
                            start=st_, stop=sp_),
                             reads=[f"vtm{sc}", f"aTs{sc}"], writes=[pgr])
                        idx += 1
                    if n > 0:
                        for dc in range(8):
                            sp_ = (idx == nmm - 1)
                            P.op("tensor", lambda e, pg=pg, dvc=dvc, dc=dc, sp_=sp_: e.matmul(
                                pg[:], state[dc][:, dvc * 128:(dvc + 1) * 128], qfm[dc][:],
                                start=False, stop=sp_),
                                 reads=[f"state{dc}", f"qfm{dc}"], writes=[pgr])
                            idx += 1
                    P.op("scalar", lambda e, pg=pg, dvc=dvc: e.activation(yfm[dvc][:], pg[:], AF.Copy),
                         reads=[pgr], writes=[f"yfm{dvc}"])
                def emit_kv_dc(dc):
                        for half in range(2):
                            pg, pgr = ps[2 + half], PS[2 + half]
                            for sc in range(4):
                                P.op("tensor", lambda e, pg=pg, dc=dc, half=half, sc=sc: e.matmul(
                                    pg[:], ktm[sc][:, dc * 128:(dc + 1) * 128],
                                    vw[sc][:, half * 512:(half + 1) * 512],
                                    start=(sc == 0), stop=(sc == 3)),
                                     reads=[f"ktm{sc}", f"vw{sc}"], writes=[pgr])
                        for half in range(2):
                            pg, pgr = ps[2 + half], PS[2 + half]
                            if n == 0:
                                P.op("scalar", lambda e, pg=pg, dc=dc, half=half: e.activation(
                                    state[dc][:, half * 512:(half + 1) * 512], pg[:], AF.Copy),
                                     reads=[pgr], writes=[f"state{dc}"])
                            else:
                                kb = (dc * 2 + half) % 2
                                P.op("scalar", lambda e, pg=pg, kb=kb: e.activation(
                                    kvst[kb][:], pg[:], AF.Copy),
                                     reads=[pgr], writes=[f"kvst{kb}"])
                                P.op("vector", lambda e, dc=dc, half=half, kb=kb: e.scalar_tensor_tensor(
                                    state[dc][:, half * 512:(half + 1) * 512],
                                    state[dc][:, half * 512:(half + 1) * 512], D512, kvst[kb][:],
                                    ALU.mult, ALU.add),
                                     reads=[f"kvst{kb}", f"state{dc}"], writes=[f"state{dc}"])
                def emit_comb_sub(sub):
                        b2 = sub % 2
                        gsubi = (r0 + sub * 128) // 128
                        P.op("gpsimd", lambda e, b2=b2, gsubi=gsubi: e.indirect_dma_start(
                            gq0[b2][:], None, obkt, IOA(gidxi[:, 2 * gsubi:2 * gsubi + 1], 0)),
                             reads=["gidxi", "obkt"], writes=[f"gq0_{b2}"], dma=True)
                        P.op("gpsimd", lambda e, b2=b2, gsubi=gsubi: e.indirect_dma_start(
                            gq1[b2][:], None, obkt, IOA(gidxi[:, 2 * gsubi + 1:2 * gsubi + 2], 0)),
                             reads=["gidxi", "obkt"], writes=[f"gq1_{b2}"], dma=True)
                        srows = shd[r0 + sub * 128:r0 + (sub + 1) * 128, :]
                        P.op("sync", lambda e, b2=b2, srows=srows: e.dma_start(out=shsb[b2][:], in_=srows),
                             reads=["shd"], writes=[f"shsb{b2}"], dma=True)
                        xrows = dram["x"][r0 + sub * 128:r0 + (sub + 1) * 128, :]
                        P.op("sync", lambda e, b2=b2, xrows=xrows: e.dma_start(out=xsb[b2][:], in_=xrows),
                             writes=[f"xsb{b2}"], dma=True)
                        P.op("vector", lambda e, b2=b2: e.tensor_tensor(
                            fsum[b2][:], gq0[b2][:], gq1[b2][:], ALU.add),
                             reads=[f"gq0_{b2}", f"gq1_{b2}"], writes=[f"fsum{b2}"])
                        P.op("vector", lambda e, b2=b2: e.tensor_tensor(
                            fsum[b2][:], fsum[b2][:], shsb[b2][:], ALU.add),
                             reads=[f"shsb{b2}", f"fsum{b2}"], writes=[f"fsum{b2}"])
                        P.op("scalar", lambda e, b2=b2: e.activation(gcb[b2][:], fsum[b2][:], AF.Silu),
                             reads=[f"fsum{b2}"], writes=[f"gcb{b2}"])
                        for half in range(2):
                            pg, pgr = ps[half], PS[half]
                            for dvc in range(8):
                                P.op("tensor", lambda e, pg=pg, sub=sub, half=half, dvc=dvc: e.matmul(
                                    pg[:], yfm[dvc][:, sub * 128:(sub + 1) * 128],
                                    wosb[:, dvc * 1024 + half * 512:dvc * 1024 + (half + 1) * 512],
                                    start=(dvc == 0), stop=(dvc == 7)),
                                     reads=[f"yfm{dvc}", "Wo"], writes=[pgr])
                            P.op("vector", lambda e, pg=pg, b2=b2, half=half: e.tensor_tensor(
                                osb[b2][:, half * 512:(half + 1) * 512], pg[:],
                                gcb[b2][:, half * 512:(half + 1) * 512], ALU.mult),
                                 reads=[pgr, f"gcb{b2}"], writes=[f"osb{b2}"])
                            P.op("vector", lambda e, b2=b2, half=half: e.tensor_tensor(
                                osb[b2][:, half * 512:(half + 1) * 512],
                                osb[b2][:, half * 512:(half + 1) * 512],
                                xsb[b2][:, half * 512:(half + 1) * 512], ALU.add),
                                 reads=[f"xsb{b2}", f"osb{b2}"], writes=[f"osb{b2}"])
                        orows = out_d[r0 + sub * 128:r0 + (sub + 1) * 128, :]
                        P.op("scalar", lambda e, b2=b2, orows=orows: e.dma_start(out=orows, in_=osb[b2][:]),
                             reads=[f"osb{b2}"], writes=["OUT"], dma=True)
                for dc in range(8):
                    emit_kv_dc(dc)
                    if dc % 2 == 1:
                        emit_comb_sub(dc // 2)
            P.barrier()
            P.emit(S0)
    return nc



_NC_CACHE = {}


def _get_nc(T):
    if T not in _NC_CACHE:
        _NC_CACHE[T] = build_nc(T)
    return _NC_CACHE[T]


def make_in_map(xb, ins):
    c = host_constants()
    return {
        "x": np.ascontiguousarray(np.asarray(xb, dtype=np.float32)),
        "wq": _lhsT_tiles(np.asarray(ins["L0_attn_Wq"], dtype=np.float32)),
        "wk": _lhsT_tiles(np.asarray(ins["L0_attn_Wk"], dtype=np.float32)),
        "wv": _rhs_tiles(np.asarray(ins["L0_attn_Wv"], dtype=np.float32)),
        "wo": _rhs_tiles(np.asarray(ins["L0_attn_Wo"], dtype=np.float32)),
        "bq": np.ascontiguousarray(np.asarray(ins["L0_attn_bq"], dtype=np.float32).reshape(8, 128).T),
        "bk": np.ascontiguousarray(np.asarray(ins["L0_attn_bk"], dtype=np.float32).reshape(8, 128).T),
        "gw": np.ascontiguousarray(
            np.asarray(ins["L0_ffn_gate_w"], dtype=np.float32).T.reshape(8, 128, 8).transpose(1, 0, 2).reshape(128, 64)),
        "w1": _w13_tiles(np.asarray(ins["L0_ffn_w1"], dtype=np.float32)),
        "w3": _w13_tiles(np.asarray(ins["L0_ffn_w3"], dtype=np.float32)),
        "w2": _w2_tiles(np.asarray(ins["L0_ffn_w2"], dtype=np.float32)),
        "w1s": _w13_tiles(np.asarray(ins["L0_ffn_w1s"], dtype=np.float32))[0],
        "w3s": _w13_tiles(np.asarray(ins["L0_ffn_w3s"], dtype=np.float32))[0],
        "w2s": _w2_tiles(np.asarray(ins["L0_ffn_w2s"], dtype=np.float32))[0],
        **c,
    }


def kernel(**inputs):
    x = np.asarray(inputs["L0_attn_x"], dtype=np.float32)
    B, T, E = x.shape
    nc = _get_nc(T)
    in_maps = [make_in_map(x[b], inputs) for b in range(B)]
    res = run_bass_kernel_spmd(nc, in_maps, core_ids=list(range(B)))
    out = np.stack([np.asarray(res.results[b]["out"]).reshape(T, E) for b in range(B)])
    return out.astype(np.float32)
